# version 06
# baseline (speedup 1.0000x reference)
import numpy as np
import ml_dtypes
import concourse.bass as bass
import concourse.mybir as mybir
from concourse.bass_utils import run_bass_kernel_spmd
from contextlib import ExitStack

F32 = mybir.dt.float32
BF16 = mybir.dt.bfloat16
U8 = mybir.dt.uint8
AF = mybir.ActivationFunctionType
ALU = mybir.AluOpType
AX = mybir.AxisListType
BF = ml_dtypes.bfloat16

NCORES = 8
T = 2048
NTT = 16
D = 1024
H = 16
DH = 64
CH = 256
NCH = 8
GN = 16.0
EPS_RMS = 1e-6
EPS_LN = 1e-5
NQ = 4
QT = 512

ENGINES = ("tensor", "vector", "scalar", "gpsimd", "sync")


def tp(r, c):
    return {} if (r == 0 and c == 0) else {"tile_position": (r, c)}


class Builder:
    def __init__(self, nc, es):
        self.nc = nc
        self.plan = []
        self.semh = {}
        self.semv = {}
        self.lastwait = {e: {} for e in ENGINES}
        self.es = es
        self.tapdeps = []
        self.phase = "P0"

    def sem(self, name):
        self.semh[name] = self.es.enter_context(self.nc.semaphore(name))
        self.semv[name] = 0

    def _waits(self, eng, deps):
        best = {}
        for d in deps:
            if d is None:
                continue
            s, v = d
            if v <= 0:
                continue
            best[s] = max(best.get(s, 0), v)
        out = []
        for s, v in best.items():
            if self.lastwait[eng].get(s, 0) < v:
                self.lastwait[eng][s] = v
                out.append((self.semh[s], v))
        return out

    def emit(self, eng, make, deps=(), inc=None, dma=False):
        waits = self._waits(eng, deps)
        step = 16 if dma else 1
        tgt = None
        semh = None
        if inc is not None:
            if inc not in self.semh:
                self.sem(inc)
            self.semv[inc] += step
            tgt = (inc, self.semv[inc])
            semh = self.semh[inc]

        def fn(E, waits=waits, make=make, semh=semh, step=step):
            for (h, v) in waits:
                E.wait_ge(h, v)
            ins = make(E)
            if semh is not None:
                ins.then_inc(semh, step)
        self.plan.append((eng, fn, self.phase))
        if inc is not None and inc.startswith("dT_"):
            self.tapdeps.append(tgt)
        return tgt

    def wait_only(self, eng, deps):
        waits = self._waits(eng, deps)
        if not waits:
            return

        def fn(E, waits=waits):
            for (h, v) in waits:
                E.wait_ge(h, v)
        self.plan.append((eng, fn, self.phase))

    def now(self, sem):
        return (sem, self.semv.get(sem, 0))


class Banks:
    def __init__(self, ps):
        self.ps = ps
        self.wdep = [None] * 8
        self.rdep = [None] * 8

    def sl(self, i, prows=128, cols=512, coff=0, poff=0):
        t = self.ps[i // 4]
        c0 = (i % 4) * 512 + coff
        return t[poff:poff + prows, c0:c0 + cols]


def host_prep(inputs):
    x = np.ascontiguousarray(np.asarray(inputs["L0_attn_x"], np.float32).reshape(16384, D))
    ws = {}
    for k in ("q", "k", "v", "g", "gt", "o"):
        ws["w" + k] = np.ascontiguousarray(np.asarray(inputs["L0_attn_W" + k], np.float32))

    def blockT(w, mch, kch):
        return np.ascontiguousarray(w.reshape(mch, 128, kch, 128).transpose(0, 3, 2, 1))

    w1b = blockT(np.asarray(inputs["L0_ffn_w1_w"], np.float32), 32, 8)
    w2b = blockT(np.asarray(inputs["L0_ffn_w2_w"], np.float32), 32, 8)
    h1b = blockT(np.asarray(inputs["L0_ffn_h1_w"], np.float32), 32, 32)
    h2b = blockT(np.asarray(inputs["L0_ffn_h2_w"], np.float32), 32, 32)
    w3b = np.ascontiguousarray(np.asarray(inputs["L0_ffn_w3_w"], np.float32).T.reshape(32, 128, 1024))
    s1h = np.ascontiguousarray(np.asarray(inputs["L0_ffn_w1_s"], np.float32).reshape(32, 128).T)
    s2h = np.ascontiguousarray(np.asarray(inputs["L0_ffn_w2_s"], np.float32).reshape(32, 128).T)
    sh1h = np.ascontiguousarray(np.asarray(inputs["L0_ffn_h1_s"], np.float32).reshape(32, 128).T)
    sh2h = np.ascontiguousarray(np.asarray(inputs["L0_ffn_h2_s"], np.float32).reshape(32, 128).T)
    s3r = np.ascontiguousarray(np.asarray(inputs["L0_ffn_w3_s"], np.float32).reshape(1, 1024))

    iden = np.eye(128, dtype=np.float32)
    mask = np.zeros((128, 512), np.float32)
    jj = np.arange(128)[:, None]
    ii = np.arange(256)[None, :]
    mask[:, 0:256] = (jj <= ii).astype(np.float32)
    mask[:, 256:512] = ((jj + 128) <= ii).astype(np.float32)

    in_maps = []
    for c in range(NCORES):
        sel = np.zeros((64, 3), np.float32)
        pos = c % 4
        if pos >= 1:
            sel[:, pos - 1] = 1.0
        m = {"x": x[T * c:T * (c + 1)],
             "w1b": w1b, "w2b": w2b, "h1b": h1b, "h2b": h2b, "w3b": w3b,
             "s1h": s1h, "s2h": s2h, "sh1h": sh1h, "sh2h": sh2h, "s3r": s3r,
             "iden": iden, "mask": mask,
             "onesall": np.ones((128, 128), np.float32),
             "o64div": np.full((128, 1), 1.0 / DH, np.float32),
             "fconst": np.repeat(np.array([[EPS_LN, EPS_RMS, 0.0, 1.0 / D,
                                            1.0 / GN, 2.0 / GN, 1.0 / DH, 64.0 * EPS_RMS]],
                                          np.float32), 128, axis=0),
             "selrep": sel}
        m.update(ws)
        in_maps.append(m)
    return in_maps


def build(taps=(), upto=99, p3sub=0):
    nc = bass.Bass("TRN2", target_bir_lowering=False)
    ctx = ExitStack()

    din = {}
    for name, shape in [
            ("x", [T, D]), ("wq", [D, D]), ("wk", [D, D]), ("wv", [D, D]),
            ("wg", [D, D]), ("wgt", [D, 16]), ("wo", [D, D]),
            ("w1b", [32, 128, 8, 128]), ("w2b", [32, 128, 8, 128]),
            ("h1b", [32, 128, 32, 128]), ("h2b", [32, 128, 32, 128]),
            ("w3b", [32, 128, 1024]),
            ("s1h", [128, 32]), ("s2h", [128, 32]), ("sh1h", [128, 32]),
            ("sh2h", [128, 32]), ("s3r", [1, 1024]),
            ("iden", [128, 128]), ("mask", [128, 512]),
            ("onesall", [128, 128]), ("o64div", [128, 1]), ("selrep", [64, 3]),
            ("fconst", [128, 8])]:
        din[name] = nc.dram_tensor(name, shape, F32, kind="ExternalInput")
    out = nc.dram_tensor("out", [T, D], F32, kind="ExternalOutput")

    xnscr_hi = nc.dram_tensor("xnscr_hi", [NQ, 128, 8 * QT], BF16, kind="Internal")
    xnscr_lo = nc.dram_tensor("xnscr_lo", [NQ, 128, 8 * QT], BF16, kind="Internal")
    kscr = nc.dram_tensor("kscr", [D, T], F32, kind="Internal")
    vscr = nc.dram_tensor("vscr", [D, T], F32, kind="Internal")
    fkscr = nc.dram_tensor("fkscr", [16, T], F32, kind="Internal")
    epsscr = nc.dram_tensor("epsscr", [16, T], F32, kind="Internal")
    accscr = nc.dram_tensor("accscr", [T, D], F32, kind="Internal")
    sinscr = nc.dram_tensor("sinscr", [64, 1024], F32, kind="Internal")
    cc_in = nc.dram_tensor("cc_in", [65, 1024], F32, kind="Internal")
    cc_out = nc.dram_tensor("cc_out", [260, 1024], F32, kind="Internal")
    ebscr = nc.dram_tensor("ebscr", [1, 128], F32, kind="Internal")

    tap_t = {}
    for tname, shape, dt in [
            ("tap_xnhi", [128, 16384], BF16), ("tap_xnlo", [128, 16384], BF16),
            ("tap_gt", [16, 2 * T], F32),
            ("tap_k", [D, T], F32), ("tap_v", [D, T], F32),
            ("tap_fk", [16, T], F32), ("tap_eps", [16, T], F32),
            ("tap_q", [128, 16384], F32), ("tap_kv", [64, 1024], F32),
            ("tap_sin", [64, 1024], F32), ("tap_o", [128, 16384], F32),
            ("tap_acc", [T, D], F32)]:
        if tname in taps:
            tap_t[tname] = nc.dram_tensor(tname, shape, dt, kind="ExternalOutput")

    ARENA_B = 204 * 1024
    arena_h = ctx.enter_context(nc.sbuf_tensor("arena", [128, ARENA_B], U8))
    base = nc.lookup_mloc(arena_h).addr

    def at(name, shape, dtype, off):
        per = int(np.prod(shape[1:])) * mybir.dt.size(dtype)
        assert off + per <= ARENA_B, f"{name}: {off}+{per} > {ARENA_B}"
        h = nc.virtual_sbuf_tensor(name, shape, dtype)
        nc._tensor(h.name, h.shape, h.dtype, type="SB")
        mloc = nc.lookup_mloc(h)
        mloc.allocated = True
        mloc.addr = base + off
        return h

    iden = at("c_iden", [128, 128], F32, 0)
    mask = at("c_mask", [128, 512], F32, 512)
    onesall = at("c_onesall", [128, 128], F32, 2560)
    o64div = at("c_o64d", [128, 1], F32, 3072)
    selrep = at("c_sel", [64, 3], F32, 3104)
    s1h = at("c_s1h", [128, 32], F32, 3136)
    s2h = at("c_s2h", [128, 32], F32, 3264)
    sh1h = at("c_sh1h", [128, 32], F32, 3392)
    sh2h = at("c_sh2h", [128, 32], F32, 3520)
    fconst = at("c_fconst", [128, 8], F32, 3648)
    expB = at("c_expB", [16, 8], F32, 3680)
    Drow = at("c_drow", [16, 1], F32, 3712)
    Drow2 = at("c_drow2", [16, 1], F32, 3744)
    for _i, _v in enumerate([EPS_LN, EPS_RMS, 0.0, 1.0 / D, 1.0 / GN, 2.0 / GN,
                             1.0 / DH, 64.0 * EPS_RMS]):
        if _v != 0.0:
            nc.const_aps.register(F32, float(_v), fconst[:, _i:_i + 1])
    rowpack = at("rowpack", [128, 2048], F32, 4096)
    rowpack2 = at("rowpack2", [128, 4096], F32, 12288)
    r_stA = rowpack[0:1, :]
    r_stB = rowpack2[0:1, 0:2048]
    r_stC = rowpack2[0:1, 2048:4096]
    r32A = rowpack[32:33, :]
    r32B = rowpack2[32:33, 0:2048]

    XN_hi = at("XN_hi", [128, 16384], BF16, 28672)
    XN_lo = at("XN_lo", [128, 16384], BF16, 61440)
    XQ = at("XQ", [128, 16384], F32, 94208)
    SCR0 = 159744
    SCRB = ARENA_B - SCR0
    assert SCRB == 49152

    WSa_hi = at("WSa_hi", [128, 8192], BF16, 94208)
    WSa_lo = at("WSa_lo", [128, 8192], BF16, 94208 + 16384)
    WSb_hi = at("WSb_hi", [128, 8192], BF16, 94208 + 32768)
    WSb_lo = at("WSb_lo", [128, 8192], BF16, 94208 + 49152)
    HA_hi = at("HA_hi", [128, 16384], BF16, 94208)
    HA_lo = at("HA_lo", [128, 16384], BF16, 94208 + 32768)
    WGg_hi = at("WGg_hi", [128, 8192], BF16, 28672)
    WGg_lo = at("WGg_lo", [128, 8192], BF16, 28672 + 16384)
    WGo_hi = at("WGo_hi", [128, 8192], BF16, 61440)
    WGo_lo = at("WGo_lo", [128, 8192], BF16, 61440 + 16384)
    B2 = at("p4_b2", [64, 1024], F32, 28672 + 16384)
    B3 = at("p4_b3", [64, 1024], F32, 28672 + 20480)
    Drep = at("p4_drep", [64, 64], F32, 28672 + 24576)
    SA_sb = at("p5_sa", [128, 512], F32, 61440 + 16384)
    o2scr = at("p5_o2", [128, 512], F32, 61440 + 18432)
    SA_sbB = at("p5_sab", [128, 512], F32, 61440 + 20480)

    ps = [ctx.enter_context(nc.psum_tensor(f"PS{i}", [128, 2048], F32)) for i in range(2)]
    bk = Banks(ps)
    b = Builder(nc, ctx)
    for s in ("pe", "dv", "sc", "gp", "cc"):
        b.sem(s)

    def mm_group(bank_i, mms, deps=(), prows=128, cols=512, coff=0, poff=0):
        outap = bk.sl(bank_i, prows, cols, coff, poff)
        alldeps = list(deps)
        if bk.rdep[bank_i] is not None:
            alldeps.append(bk.rdep[bank_i])
        n = len(mms)
        dep = None
        for i, m in enumerate(mms):
            dep = b.emit("tensor",
                         (lambda E, m=m, i=i, n=n, outap=outap: m(E, outap, i == 0, i == n - 1)),
                         deps=alldeps if i == 0 else (),
                         inc="pe" if i == n - 1 else None)
        bk.wdep[bank_i] = dep
        return dep

    def read_bank(bank_i, eng, make, deps=()):
        inc = {"vector": "dv", "scalar": "sc", "gpsimd": "gp"}[eng]
        dep = b.emit(eng, make, deps=list(deps) + [bk.wdep[bank_i]], inc=inc)
        bk.rdep[bank_i] = dep
        return dep

    def _ret():
        b.wait_only("sync", b.tapdeps
                    + [b.now("pe"), b.now("dv"), b.now("sc"), b.now("gp"), b.now("cc")])
        return _finish(nc, ctx, b)

    cdep = None
    for name, hdl in [("iden", iden), ("mask", mask), ("onesall", onesall),
                      ("o64div", o64div), ("selrep", selrep), ("s1h", s1h),
                      ("s2h", s2h), ("sh1h", sh1h), ("sh2h", sh2h), ("fconst", fconst)]:
        cdep = b.emit("sync", lambda E, d=din[name], h=hdl:
                      E.dma_start(out=h[:], in_=d.ap()[:]), inc="dCst", dma=True)
    for _e in ("tensor", "vector", "scalar", "gpsimd"):
        b.wait_only(_e, [cdep])

    b.phase = "P1"

    t_tile = at("p1_t", [128, 1024], F32, SCR0)
    stats = at("p1_stats", [128, 8], F32, SCR0 + 4096)
    sqscr = at("p1_sq", [128, 1024], F32, SCR0 + 4128)

    x_deps = []
    for t in range(NTT):
        x_deps.append(b.emit("sync", lambda E, t=t:
                             E.dma_start(out=XQ[:, 1024 * t:1024 * (t + 1)],
                                         in_=din["x"].ap()[128 * t:128 * (t + 1), :]),
                             inc="dX", dma=True))
    x_all = x_deps[-1]

    XNhi3 = XN_hi[:].rearrange("p (j c) -> p j c", c=2048)
    XNlo3 = XN_lo[:].rearrange("p (j c) -> p j c", c=2048)
    xn_done = []
    prev_ev = {0: None, 1: None}
    for t in range(NTT):
        xs = XQ[:, 1024 * t:1024 * (t + 1)]
        dx = [x_all]
        d1 = b.emit("vector", lambda E, xs=xs: E.reduce_sum(stats[:, 0:1], xs, AX.X),
                    deps=dx, inc="dv")
        d2 = b.emit("scalar", lambda E, xs=xs: E.activation(sqscr[:], xs, AF.Square,
                                                            accum_out=stats[:, 1:2]),
                    deps=dx, inc="sc")
        d3 = b.emit("vector", lambda E: E.tensor_scalar(stats[:, 2:3], stats[:, 0:1],
                                                        1.0 / D, None, ALU.mult),
                    deps=[d1], inc="dv")
        d4 = b.emit("vector", lambda E: E.tensor_tensor(stats[:, 3:4], stats[:, 2:3],
                                                        stats[:, 2:3], ALU.mult),
                    deps=[d3], inc="dv")
        d5 = b.emit("vector", lambda E: E.scalar_tensor_tensor(stats[:, 4:5], stats[:, 1:2],
                                                               1.0 / D, stats[:, 3:4],
                                                               ALU.mult, ALU.subtract),
                    deps=[d2, d4], inc="dv")
        d6 = b.emit("scalar", lambda E: E.activation(stats[:, 5:6], stats[:, 4:5],
                                                     AF.Sqrt, bias=EPS_LN),
                    deps=[d5], inc="sc")
        b.emit("vector", lambda E: E.drain(), deps=[d6])
        d7 = b.emit("vector", lambda E: E.reciprocal(stats[:, 6:7], stats[:, 5:6]),
                    deps=[d6], inc="dv")
        d8 = b.emit("vector", lambda E, xs=xs: E.tensor_scalar(t_tile[:], xs, stats[:, 2:3],
                                                               stats[:, 6:7], ALU.subtract,
                                                               ALU.mult),
                    deps=[d7], inc="dv")
        for half in range(2):
            bi = half
            deps_tr = [d8, cdep]
            if prev_ev[half] is not None:
                deps_tr.append(prev_ev[half])
            if bk.rdep[bi] is not None:
                deps_tr.append(bk.rdep[bi])
            dtr = None
            for jj in range(4):
                j = half * 4 + jj
                dtr = b.emit("tensor", lambda E, j=j, jj=jj, bi=bi:
                             E.transpose(bk.sl(bi, 128, 128, 128 * jj),
                                         t_tile[:, 128 * j:128 * (j + 1)], iden[:]),
                             deps=deps_tr if jj == 0 else (),
                             inc="pe" if jj == 3 else None)
            bk.wdep[bi] = dtr
            src3 = bk.sl(bi, 128, 512).rearrange("p (j c) -> p j c", c=128)
            dhi = read_bank(bi, "scalar", lambda E, half=half, t=t, src3=src3:
                            E.activation(XNhi3[:, 4 * half:4 * half + 4, 128 * t:128 * (t + 1)],
                                         src3, AF.Copy))
            dlo = read_bank(bi, "vector", lambda E, half=half, t=t, src3=src3:
                            E.tensor_tensor(XNlo3[:, 4 * half:4 * half + 4, 128 * t:128 * (t + 1)],
                                            src3,
                                            XNhi3[:, 4 * half:4 * half + 4, 128 * t:128 * (t + 1)],
                                            ALU.subtract), deps=[dhi])
            prev_ev[half] = dlo
            xn_done.append(dlo)

    xn_stash = []
    for q in range(NQ):
        for (src3, dst) in ((XNhi3, xnscr_hi), (XNlo3, xnscr_lo)):
            xn_stash.append(b.emit("scalar", lambda E, q=q, src3=src3, dst=dst:
                                   E.dma_start(out=dst.ap()[q],
                                               in_=src3[:, :, QT * q:QT * (q + 1)]),
                                   deps=xn_done, inc="dStXn", dma=True))
    xn_tap_deps = []
    if "tap_xnhi" in tap_t:
        xn_tap_deps.append(b.emit("sync", lambda E:
                                  E.dma_start(out=tap_t["tap_xnhi"].ap()[:], in_=XN_hi[:]),
                                  deps=xn_done, inc="dT_xnhi", dma=True))
    if "tap_xnlo" in tap_t:
        xn_tap_deps.append(b.emit("sync", lambda E:
                                  E.dma_start(out=tap_t["tap_xnlo"].ap()[:], in_=XN_lo[:]),
                                  deps=xn_done, inc="dT_xnlo", dma=True))
    if upto < 2:
        return _ret()

    b.phase = "P1b"

    WA = at("gt_wa", [16, T], F32, SCR0)
    WB = at("gt_wb", [16, T], F32, SCR0 + 8192)
    WC = at("gt_wc", [16, T], F32, SCR0 + 16384)
    wgt_hi = at("wgt_hi", [128, 128], BF16, SCR0 + 24576)
    wgt_lo = at("wgt_lo", [128, 128], BF16, SCR0 + 24832)
    wgt_st = at("wgt_st", [128, 16], F32, SCR0 + 25088)
    fsl = [at(f"gt_fsl{i}", [16, 512], F32, SCR0 + 25600 + 2048 * i) for i in range(2)]

    wg_deps = []
    prev_lo = None
    for k in range(8):
        dh_ = b.emit("gpsimd", lambda E, k=k:
                     E.dma_start(out=wgt_hi[:, 16 * k:16 * (k + 1)],
                                 in_=din["wgt"].ap()[128 * k:128 * (k + 1), :]),
                     inc="dWgt", dma=True)
        ds_ = b.emit("gpsimd", lambda E, k=k:
                     E.dma_start(out=wgt_st[:], in_=din["wgt"].ap()[128 * k:128 * (k + 1), :]),
                     deps=[prev_lo] if prev_lo else (), inc="dWgt", dma=True)
        prev_lo = b.emit("gpsimd", lambda E, k=k:
                         E.tensor_tensor(wgt_lo[:, 16 * k:16 * (k + 1)], wgt_st[:],
                                         wgt_hi[:, 16 * k:16 * (k + 1)], ALU.subtract),
                         deps=[dh_, ds_], inc="gp")
        wg_deps.append(prev_lo)

    sig_deps = []
    for n in range(4):
        bi = 2 + (n % 2)
        mms = []
        for k in range(8):
            for wt, xt in ((wgt_hi, XN_hi), (wgt_hi, XN_lo), (wgt_lo, XN_hi)):
                mms.append(lambda E, o, s0, s1, n=n, k=k, wt=wt, xt=xt:
                           E.matmul(o, wt[:, 16 * k:16 * (k + 1)],
                                    xt[:, 2048 * k + 512 * n:2048 * k + 512 * (n + 1)],
                                    start=s0, stop=s1))
        mm_group(bi, mms, deps=wg_deps + xn_done, prows=16)
        sig_deps.append(read_bank(bi, "scalar", lambda E, n=n, bi=bi:
                                  E.activation(WC[:, 512 * n:512 * (n + 1)],
                                               bk.sl(bi, 16), AF.Sigmoid),
                                  deps=xn_done))
    dln = b.emit("scalar", lambda E: E.activation(WB[:], WC[:], AF.Ln),
                 deps=sig_deps + xn_done, inc="sc")

    cur, nxt = WB, WA
    dcur_l = [dln]
    for s in range(8):
        step = 1 << s
        ds_list = []
        for c in range(NCH):
            b0 = CH * c
            ds_list.append(b.emit("vector", lambda E, cur=cur, nxt=nxt, b0=b0, step=step:
                                  E.tensor_tensor(nxt[:, b0 + step:b0 + CH],
                                                  cur[:, b0 + step:b0 + CH],
                                                  cur[:, b0:b0 + CH - step], ALU.add),
                                  deps=dcur_l, inc="dv"))
            ds_list.append(b.emit("scalar", lambda E, cur=cur, nxt=nxt, b0=b0, step=step:
                                  E.activation(nxt[:, b0:b0 + step], cur[:, b0:b0 + step],
                                               AF.Copy),
                                  deps=dcur_l, inc="sc"))
        dcur_l = ds_list
        cur, nxt = nxt, cur
    C = cur
    assert C is WB
    dC = b.emit("vector", lambda E: E.tensor_copy(Drow[:], C[:, T - 1:T]),
                deps=dcur_l, inc="dv")
    Cends = C[:].rearrange("p (c z) -> p c z", z=CH)[:, :, CH - 1:CH].rearrange("p c z -> p (c z)")
    dEB = b.emit("scalar", lambda E: E.activation(expB[:], Cends, AF.Exp, scale=1.0 / GN),
                 deps=[dC], inc="sc")
    dBs = b.emit("vector", lambda E: E.reduce_sum(Drow[:], Cends, AX.X), deps=[dC], inc="dv")
    dD = b.emit("scalar", lambda E: E.activation(Drow2[:], Drow[:], AF.Exp, scale=1.0 / GN),
                deps=[dBs], inc="sc")
    dbx = []
    for c in range(NCH):
        dbx.append(b.emit("vector", lambda E, c=c:
                          E.tensor_scalar(WC[:, CH * c:CH * (c + 1)], C[:, CH * c:CH * (c + 1)],
                                          0.0, C[:, CH * c + CH - 1:CH * c + CH],
                                          ALU.mult, ALU.add),
                          deps=[dC], inc="dv"))
    dU = b.emit("vector", lambda E: E.tensor_tensor(WA[:], WC[:], C[:], ALU.subtract),
                deps=dbx, inc="dv")
    Ubuf = WA
    debf0 = b.emit("sync", lambda E: E.dma_start(
        out=ebscr.ap().rearrange("z (p c) -> (z p) c", p=16), in_=expB[:]),
        deps=[dEB], inc="dEbf", dma=True)
    if "tap_gt" in tap_t:
        b.emit("sync", lambda E: E.dma_start(out=tap_t["tap_gt"].ap()[:, 0:T], in_=WA[:]),
               deps=[dU], inc="dT_gt", dma=True)
        b.emit("sync", lambda E: E.dma_start(out=tap_t["tap_gt"].ap()[:, T:2 * T], in_=WB[:]),
               deps=[dU], inc="dT_gt", dma=True)
    if upto < 3:
        return _ret()

    b.phase = "P2"

    kch = [at(f"p2_kc{i}", [128, 2048], F32, SCR0 + 8192 * (i + 1)) for i in range(2)]
    sq2 = at("p2_sq", [128, 512], F32, SCR0 + 40960)
    wst2 = [at(f"p2_wst{i}", [128, 512], F32, SCR0 + 45056 + 2048 * i) for i in range(2)]
    statrow = [r_stA, r_stB]
    stat_state = {"i": 0, "dep": None}

    def stat_reset():
        stat_state["i"] = 0
        stat_state["dep"] = b.emit("vector", lambda E: E.memset(statrow[0][:, :], 0.0),
                                   deps=[stat_state["dep"]], inc="dv")

    wst_war = [None, None]

    def load_whilo(wname, whi, wlo, deps=()):
        semh = "dwh_" + wname
        seml = "dws_" + wname
        dhk = []
        for k in range(8):
            dhk.append(b.emit("gpsimd", lambda E, k=k, whi=whi:
                              E.dma_start(out=whi[:, 1024 * k:1024 * (k + 1)],
                                          in_=din[wname].ap()[128 * k:128 * (k + 1), :]),
                              deps=deps, inc=semh, dma=True))
        lastv = None
        for j in range(16):
            k, h2 = j // 2, j % 2
            st = wst2[j % 2]
            ds_ = b.emit("sync", lambda E, k=k, h2=h2, st=st:
                         E.dma_start(out=st[:],
                                     in_=din[wname].ap()[128 * k:128 * (k + 1),
                                                         512 * h2:512 * (h2 + 1)]),
                         deps=list(deps) + [wst_war[j % 2]], inc=seml + str(j % 2),
                         dma=True)
            lastv = b.emit("vector", lambda E, k=k, h2=h2, st=st, whi=whi, wlo=wlo:
                           E.tensor_tensor(wlo[:, 1024 * k + 512 * h2:1024 * k + 512 * (h2 + 1)],
                                           st[:],
                                           whi[:, 1024 * k + 512 * h2:1024 * k + 512 * (h2 + 1)],
                                           ALU.subtract),
                           deps=[ds_, dhk[-1]] + list(deps), inc="dv")
            wst_war[j % 2] = lastv
        return [lastv, dhk[-1]]

    def load_whi_only(wname, whi, deps=()):
        sem = "dwh_" + wname
        dh_ = None
        for k in range(8):
            dh_ = b.emit("gpsimd", lambda E, k=k, whi=whi:
                         E.dma_start(out=whi[:, 1024 * k:1024 * (k + 1)],
                                     in_=din[wname].ap()[128 * k:128 * (k + 1), :]),
                         deps=deps, inc=sem, dma=True)
        return dh_

    def load_wlo_only(wname, whi, wlo, stpair, war, deps=()):
        sem = "dwl_" + wname
        lastv = None
        for j in range(16):
            k, h2 = j // 2, j % 2
            st = stpair[j % 2]
            ds_ = b.emit("sync", lambda E, k=k, h2=h2, st=st:
                         E.dma_start(out=st[:],
                                     in_=din[wname].ap()[128 * k:128 * (k + 1),
                                                         512 * h2:512 * (h2 + 1)]),
                         deps=list(deps) + [war[j % 2]], inc=sem + str(j % 2), dma=True)
            lastv = b.emit("vector", lambda E, k=k, h2=h2, st=st, whi=whi, wlo=wlo:
                           E.tensor_tensor(wlo[:, 1024 * k + 512 * h2:1024 * k + 512 * (h2 + 1)],
                                           st[:],
                                           whi[:, 1024 * k + 512 * h2:1024 * k + 512 * (h2 + 1)],
                                           ALU.subtract),
                           deps=[ds_] + list(deps), inc="dv")
            war[j % 2] = lastv
        return lastv

    dWk = load_whilo("wk", WSa_hi, WSa_lo, deps=xn_done)
    dWv = load_whilo("wv", WSb_hi, WSb_lo, deps=xn_done)

    def proj_fm(whi, wlo, wdep, stash_dram, semp, war0=(None, None)):
        stash_deps = []
        stash_war = list(war0)
        for m in range(8):
            buf = kch[m % 2]
            base_bi = 0 if m % 2 == 0 else 4
            ev_deps = []
            sq_deps = []
            for n in range(4):
                bi = base_bi + n
                mms = []
                for k in range(8):
                    for wt, xt in ((whi, XN_hi), (whi, XN_lo), (wlo, XN_hi)):
                        mms.append(lambda E, o, s0, s1, k=k, m=m, n=n, wt=wt, xt=xt:
                                   E.matmul(o, wt[:, 1024 * k + 128 * m:1024 * k + 128 * (m + 1)],
                                            xt[:, 2048 * k + 512 * n:2048 * k + 512 * (n + 1)],
                                            start=s0, stop=s1))
                mm_group(bi, mms, deps=list(wdep) + xn_done)
                dev = read_bank(bi, "scalar", lambda E, n=n, bi=bi, buf=buf:
                                E.activation(buf[:, 512 * n:512 * (n + 1)], bk.sl(bi), AF.Copy),
                                deps=[dC, dU] + b.tapdeps
                                + ([stash_war[m % 2]] if stash_war[m % 2] else []))
                ev_deps.append(dev)
                dsq = b.emit("scalar", lambda E, n=n, buf=buf:
                             E.activation(sq2[:], buf[:, 512 * n:512 * (n + 1)], AF.Square),
                             deps=[dev] + sq_deps[-1:], inc="sc")
                mm_group(bi, [lambda E, o, s0, s1:
                              E.matmul(o, onesall[:, 0:1], sq2[:], start=s0, stop=s1)],
                         deps=[dsq], prows=1)
                i = stat_state["i"]
                src_r, dst_r = statrow[i % 2], statrow[1 - i % 2]
                sdep = read_bank(bi, "vector",
                                 lambda E, n=n, bi=bi, src_r=src_r, dst_r=dst_r:
                                 E.tensor_tensor(dst_r[:, 512 * n:512 * (n + 1)],
                                                 src_r[:, 512 * n:512 * (n + 1)],
                                                 bk.sl(bi, 1), ALU.add),
                                 deps=[stat_state["dep"]])
                sq_deps = [sdep]
                stat_state["dep"] = sdep
            stat_state["i"] += 1
            stash_deps.append(b.emit("scalar", lambda E, m=m, buf=buf, stash_dram=stash_dram:
                                     E.dma_start(out=stash_dram.ap()[128 * m:128 * (m + 1), :],
                                                 in_=buf[:]),
                                     deps=ev_deps, inc=semp + str(m % 2), dma=True))
            stash_war[m % 2] = stash_deps[-1]
        return stash_deps

    stat_reset()
    k_stash = proj_fm(WSa_hi, WSa_lo, dWk, kscr, "dStK")
    kfinal = statrow[1 - (stat_state["i"] - 1) % 2]
    d_kf = stat_state["dep"]
    d_sk = b.emit("scalar", lambda E: E.activation(r_stC[:, :], kfinal[:, :], AF.Sqrt,
                                                   bias=EPS_RMS, scale=1.0 / D),
                  deps=[d_kf], inc="sc")
    stat_state["dep"] = d_sk
    stat_reset()
    v_stash = proj_fm(WSb_hi, WSb_lo, dWv, vscr, "dStV", war0=k_stash[-2:])
    vfinal = statrow[1 - (stat_state["i"] - 1) % 2]
    vother = statrow[(stat_state["i"] - 1) % 2]
    d_vf = stat_state["dep"]
    d_sv = b.emit("scalar", lambda E: E.activation(vother[:, :], vfinal[:, :], AF.Sqrt,
                                                   bias=EPS_RMS, scale=1.0 / D),
                  deps=[d_vf, d_sk], inc="sc")
    d_m1 = b.emit("vector", lambda E: E.tensor_tensor(vfinal[:, :], r_stC[:, :],
                                                      vother[:, :], ALU.mult),
                  deps=[d_sv], inc="dv")
    b.emit("vector", lambda E: E.drain(), deps=[d_m1])
    d_rkv = b.emit("vector", lambda E: E.reciprocal(r_stC[:, :], vfinal[:, :]),
                   deps=[d_m1], inc="dv")
    fk_st = []
    for n in range(4):
        bi = 2 + (n % 2)
        mm_group(bi, [lambda E, o, s0, s1, n=n:
                      E.matmul(o, onesall[0:1, 0:16], r_stC[:, 512 * n:512 * (n + 1)],
                               start=s0, stop=s1)],
                 deps=[d_rkv], prows=16)
        de1 = b.emit("scalar", lambda E, n=n:
                     E.activation(fsl[0][:], Ubuf[:, 512 * n:512 * (n + 1)], AF.Exp,
                                  scale=1.0 / GN),
                     deps=[dU] + fk_st[-1:], inc="sc")
        dfk = read_bank(bi, "vector", lambda E, n=n, bi=bi:
                        E.tensor_tensor(fsl[1][:], fsl[0][:], bk.sl(bi, 16), ALU.mult),
                        deps=[de1] + fk_st[-1:])
        fk_st.append(b.emit("scalar", lambda E, n=n:
                            E.dma_start(out=fkscr.ap()[:, 512 * n:512 * (n + 1)],
                                        in_=fsl[1][:]),
                            deps=[dfk], inc="dFkSt", dma=True))
    d_fkd = fk_st[-1]
    if "tap_fk" in tap_t:
        b.emit("sync", lambda E: E.dma_start(out=tap_t["tap_fk"].ap()[:], in_=fkscr.ap()[:]),
               deps=[d_fkd], inc="dT_fk", dma=True)
    if "tap_k" in tap_t:
        b.emit("sync", lambda E: E.dma_start(out=tap_t["tap_k"].ap()[:], in_=kscr.ap()[:]),
               deps=k_stash[-2:], inc="dT_k", dma=True)
    if "tap_v" in tap_t:
        b.emit("sync", lambda E: E.dma_start(out=tap_t["tap_v"].ap()[:], in_=vscr.ap()[:]),
               deps=v_stash[-2:], inc="dT_v", dma=True)
    if upto < 4:
        return _ret()

    b.phase = "P4a"

    WQ_hi = at("WQ_hi", [128, 8192], BF16, SCR0 + 8192)
    WQ_lo = at("WQ_lo", [128, 8192], BF16, SCR0 + 24576)
    dWq = load_whilo("wq", WQ_hi, WQ_lo,
                     deps=v_stash[-2:] + k_stash[-2:] + [dC, dU, d_fkd])
    dWq_l = list(dWq)
    wq_stage_dead = [wst_war[0], wst_war[1]]
    stat_state["dep"] = d_rkv
    stat_reset()
    q_sq_deps = []
    q_done = []
    for m in range(8):
        base_bi = 0 if m % 2 == 0 else 4
        for n in range(4):
            bi = base_bi + n
            mms = []
            for k in range(8):
                for wt, xt in ((WQ_hi, XN_hi), (WQ_hi, XN_lo), (WQ_lo, XN_hi)):
                    mms.append(lambda E, o, s0, s1, k=k, m=m, n=n, wt=wt, xt=xt:
                               E.matmul(o, wt[:, 1024 * k + 128 * m:1024 * k + 128 * (m + 1)],
                                        xt[:, 2048 * k + 512 * n:2048 * k + 512 * (n + 1)],
                                        start=s0, stop=s1))
            mm_group(bi, mms, deps=dWq_l)
            dev = read_bank(bi, "scalar", lambda E, m=m, n=n, bi=bi:
                            E.activation(XQ[:, 2048 * m + 512 * n:2048 * m + 512 * (n + 1)],
                                         bk.sl(bi), AF.Copy))
            q_done.append(dev)
            dsq = b.emit("scalar", lambda E, m=m, n=n:
                         E.activation(sq2[:], XQ[:, 2048 * m + 512 * n:2048 * m + 512 * (n + 1)],
                                      AF.Square), deps=[dev] + q_sq_deps[-1:], inc="sc")
            mm_group(bi, [lambda E, o, s0, s1:
                          E.matmul(o, onesall[:, 0:1], sq2[:], start=s0, stop=s1)],
                     deps=[dsq], prows=1)
            i = stat_state["i"]
            src_r, dst_r = statrow[i % 2], statrow[1 - i % 2]
            sdep = read_bank(bi, "vector", lambda E, n=n, bi=bi, src_r=src_r, dst_r=dst_r:
                             E.tensor_tensor(dst_r[:, 512 * n:512 * (n + 1)],
                                             src_r[:, 512 * n:512 * (n + 1)],
                                             bk.sl(bi, 1), ALU.add),
                             deps=[stat_state["dep"]])
            q_sq_deps = [sdep]
            stat_state["dep"] = sdep
        stat_state["i"] += 1
    qfinal = statrow[1 - (stat_state["i"] - 1) % 2]
    d_qf = stat_state["dep"]
    d_mp = b.emit("vector", lambda E: E.tensor_scalar(r_stC[:, :], qfinal[:, :], 1.0 / D,
                                                      EPS_RMS, ALU.mult, ALU.add),
                  deps=[d_qf, d_fkd], inc="dv")
    eps_st = []
    for n in range(4):
        bi = 2 + (n % 2)
        mm_group(bi, [lambda E, o, s0, s1, n=n:
                      E.matmul(o, onesall[0:1, 0:16], r_stC[:, 512 * n:512 * (n + 1)],
                               start=s0, stop=s1)],
                 deps=[d_mp], prows=16)
        de2 = b.emit("scalar", lambda E, n=n:
                     E.activation(fsl[0][:], Ubuf[:, 512 * n:512 * (n + 1)], AF.Exp,
                                  scale=2.0 / GN),
                     deps=[dU, d_fkd] + eps_st[-1:], inc="sc")
        dep_ = read_bank(bi, "vector", lambda E, n=n, bi=bi:
                         E.scalar_tensor_tensor(fsl[1][:], fsl[0][:], float(DH) * EPS_RMS,
                                                bk.sl(bi, 16), ALU.mult, ALU.mult),
                         deps=[de2] + eps_st[-1:])
        eps_st.append(b.emit("scalar", lambda E, n=n:
                             E.dma_start(out=epsscr.ap()[:, 512 * n:512 * (n + 1)],
                                         in_=fsl[1][:]),
                             deps=[dep_], inc="dEpSt", dma=True))
    d_epsd = eps_st[-1]
    tap_q_dep = None
    if "tap_eps" in tap_t:
        b.emit("sync", lambda E: E.dma_start(out=tap_t["tap_eps"].ap()[:], in_=epsscr.ap()[:]),
               deps=[d_epsd], inc="dT_eps", dma=True)
    if "tap_q" in tap_t:
        tap_q_dep = b.emit("sync", lambda E: E.dma_start(out=tap_t["tap_q"].ap()[:], in_=XQ[:]),
                           deps=q_done, inc="dT_q", dma=True)
    dWgh = load_whi_only("wg", WGg_hi, deps=q_done + xn_stash[-1:] + xn_tap_deps)
    dWoh = load_whi_only("wo", WGo_hi, deps=q_done + xn_stash[-1:] + xn_tap_deps)
    if upto < 5:
        return _ret()

    b.phase = "P3"

    fb = [at(f"p3_fb{i}", [128, 2048], F32, SCR0 + 8192 * i) for i in range(2)]
    kd_fm = at("p3_kdfm", [128, 2048], F32, SCR0 + 16384)
    kd_tm = at("p3_kdtm", [128, 1024], F32, SCR0 + 24576)
    v_tm = at("p3_vtm", [128, 1024], F32, SCR0 + 28672)
    KV_all = at("p3_kv", [64, 1024], F32, SCR0 + 32768)
    Sbuf = [at(f"p3_s{i}", [128, 64], F32, SCR0 + 36864 + 256 * i) for i in range(2)]
    eBr_all = at("p3_ebr", [128, 128], F32, SCR0 + 37376)
    Ss_all = at("p3_ssall", [128, 512], F32, SCR0 + 37888)
    o_head = at("p5_oh", [128, 2048], F32, SCR0 + 39936)
    Dflat = at("p3_dflat", [1, 1024], F32, SCR0 + 39936)
    KVg = at("p4_kvg", [64, 1024], F32, SCR0 + 39936)
    B1 = at("p4_b1", [64, 1024], F32, SCR0 + 44032)

    debf = b.emit("sync", lambda E: E.dma_start(out=r32A[:, 0:128], in_=ebscr.ap()),
                  deps=[debf0, d_epsd], inc="dEbf", dma=True)
    mm_group(6, [lambda E, o, s0, s1:
                 E.matmul(o, onesall[32:33, 0:64], r32A[:, 0:128], start=s0, stop=s1,
                          **tp(32, 0))],
             deps=[debf], prows=64, cols=128)
    read_bank(6, "vector", lambda E: E.tensor_copy(eBr_all[0:64, :], bk.sl(6, 64, 128)),
              deps=[d_epsd])
    mm_group(7, [lambda E, o, s0, s1:
                 E.matmul(o, onesall[32:33, 0:64], r32A[:, 0:128], start=s0, stop=s1,
                          **tp(32, 64))],
             deps=[debf], prows=64, cols=128, poff=64)
    dEBr = read_bank(7, "vector", lambda E:
                     E.tensor_copy(eBr_all[64:128, :], bk.sl(7, 64, 128, poff=64)),
                     deps=[d_epsd])
    if p3sub == 1:
        return _ret()

    hp_war = [None, None]
    stg_war = [None]
    kd_war = [None]
    vtm_war = [None]

    def head_prep(h, dep_extra=()):
        qrow = 64 * (h % 2)
        vrow = 64 - qrow
        buf = fb[h % 2]
        war = list(hp_war[h % 2]) if hp_war[h % 2] else []
        dk = b.emit("sync", lambda E, h=h, buf=buf, qrow=qrow:
                    E.dma_start(out=buf[qrow:qrow + 64, :],
                                in_=kscr.ap()[64 * h:64 * (h + 1), :]),
                    deps=k_stash[-2:] + v_stash[-2:] + war + [d_epsd] + b.tapdeps
                    + list(dep_extra), inc="dFk%d" % (h % 2), dma=True)
        dv_ = b.emit("sync", lambda E, h=h, buf=buf, vrow=vrow:
                     E.dma_start(out=buf[vrow:vrow + 64, :],
                                 in_=vscr.ap()[64 * h:64 * (h + 1), :]),
                     deps=k_stash[-2:] + v_stash[-2:] + war + [d_epsd] + list(dep_extra),
                     inc="dFv%d" % (h % 2), dma=True)
        dstg = b.emit("sync", lambda E, h=h:
                      E.dma_start(out=r_stA[:, :], in_=fkscr.ap()[h:h + 1, :]),
                      deps=[d_fkd, d_epsd] + ([stg_war[0]] if stg_war[0] else []),
                      inc="dFkR", dma=True)
        dkds = []
        for n in range(4):
            bi = 2 + (n % 2)
            mm_group(bi, [lambda E, o, s0, s1, n=n, qrow=qrow:
                          E.matmul(o, onesall[0:1, 0:64], r_stA[:, 512 * n:512 * (n + 1)],
                                   start=s0, stop=s1, **tp(0, qrow))],
                     deps=[dstg], prows=64, poff=qrow)
            dkds.append(read_bank(bi, "vector",
                                  lambda E, n=n, bi=bi, buf=buf, qrow=qrow:
                                  E.tensor_tensor(kd_fm[qrow:qrow + 64, 512 * n:512 * (n + 1)],
                                                  buf[qrow:qrow + 64, 512 * n:512 * (n + 1)],
                                                  bk.sl(bi, 64, 512, poff=qrow), ALU.mult),
                                  deps=[dk] + ([kd_war[0]] if kd_war[0] else [])))
        stg_war[0] = dkds[-1]
        dkdtm = []
        for half in range(2):
            bi = half
            deps_tr = dkds + ([bk.rdep[bi]] if bk.rdep[bi] else [])
            dtr = None
            for jj in range(8):
                tt8 = half * 8 + jj
                dtr = b.emit("tensor", lambda E, tt8=tt8, jj=jj, bi=bi, qrow=qrow:
                             E.transpose(bk.sl(bi, 128, 64, 64 * jj),
                                         kd_fm[qrow:qrow + 64, 128 * tt8:128 * (tt8 + 1)],
                                         iden[qrow:qrow + 64, qrow:qrow + 64]),
                             deps=deps_tr if jj == 0 else (), inc="pe" if jj == 7 else None)
            bk.wdep[bi] = dtr
            dkdtm.append(read_bank(bi, "scalar", lambda E, half=half, bi=bi:
                                   E.activation(kd_tm[:, 512 * half:512 * (half + 1)],
                                                bk.sl(bi), AF.Copy)))
        dvtm = []
        for half in range(2):
            bi = 2 + half
            deps_tr = [dv_] + ([bk.rdep[bi]] if bk.rdep[bi] else []) + \
                ([vtm_war[0]] if vtm_war[0] else [])
            dtr = None
            for jj in range(8):
                tt8 = half * 8 + jj
                dtr = b.emit("tensor", lambda E, tt8=tt8, jj=jj, bi=bi, buf=buf, vrow=vrow:
                             E.transpose(bk.sl(bi, 128, 64, 64 * jj),
                                         buf[vrow:vrow + 64, 128 * tt8:128 * (tt8 + 1)],
                                         iden[vrow:vrow + 64, vrow:vrow + 64]),
                             deps=deps_tr if jj == 0 else (), inc="pe" if jj == 7 else None)
            bk.wdep[bi] = dtr
            dvtm.append(read_bank(bi, "scalar", lambda E, half=half, bi=bi:
                                  E.activation(v_tm[:, 512 * half:512 * (half + 1)],
                                               bk.sl(bi), AF.Copy)))
        return {"buf": buf, "kdtm": dkdtm, "vtm": dvtm, "k": dk, "v": dv_,
                "kdfm": dkds, "qrow": qrow}

    dKV = []
    sw1_war = [None]
    for h in range(H):
        hp = head_prep(h)
        qrow = hp["qrow"]
        dS = b.emit("vector", lambda E, qrow=qrow: E.memset(Sbuf[0][qrow:qrow + 64, :], 0.0),
                    deps=[sw1_war[0]] if sw1_war[0] else [d_epsd], inc="dv")
        alld = hp["kdtm"] + hp["vtm"] + ([bk.rdep[7]] if bk.rdep[7] else [])
        pe_dep = None
        for c in range(NCH):
            for j in range(2):
                pe_dep = b.emit("tensor", lambda E, c=c, j=j, qrow=qrow:
                                E.matmul(bk.sl(7, 64, 64, 64 * c, qrow),
                                         kd_tm[:, 64 * (2 * c + j):64 * (2 * c + j + 1)],
                                         v_tm[:, 64 * (2 * c + j):64 * (2 * c + j + 1)],
                                         start=(j == 0), stop=(j == 1), **tp(0, qrow)),
                                deps=alld if (c == 0 and j == 0) else (),
                                inc="pe" if (c == NCH - 1 and j == 1) else None)
        bk.wdep[7] = pe_dep
        dprev = dS
        for c in range(NCH):
            dts = b.emit("vector", lambda E, c=c, h=h, qrow=qrow:
                         E.tensor_scalar(Sbuf[1][qrow:qrow + 64, :],
                                         Sbuf[0][qrow:qrow + 64, :],
                                         eBr_all[qrow:qrow + 64, 8 * h + c:8 * h + c + 1],
                                         None, ALU.mult),
                         deps=[dprev, dEBr], inc="dv")
            dprev = b.emit("vector", lambda E, c=c, qrow=qrow:
                           E.tensor_tensor(Sbuf[0][qrow:qrow + 64, :],
                                           Sbuf[1][qrow:qrow + 64, :],
                                           bk.sl(7, 64, 64, 64 * c, qrow), ALU.add),
                           deps=[dts, pe_dep], inc="dv")
        bk.rdep[7] = dprev
        dcp = b.emit("scalar", lambda E, h=h, qrow=qrow:
                     E.activation(KV_all[:, 64 * h:64 * (h + 1)],
                                  Sbuf[0][qrow:qrow + 64, :], AF.Copy),
                     deps=[dprev], inc="sc")
        dKV.append(dcp)
        sw1_war[0] = dcp
        hp_war[h % 2] = [hp["vtm"][-1], hp["kdfm"][-1]]
        kd_war[0] = hp["kdtm"][-1]
        vtm_war[0] = hp["kdtm"][-1]
        if p3sub == 2 and h == 0:
            return _ret()
    if p3sub == 3:
        return _ret()

    dz_df = b.emit("vector", lambda E: E.memset(Dflat[:], 0.0), deps=[d_epsd], inc="dv")
    mm_group(6, [lambda E, o, s0, s1: E.transpose(o, Drow2[:], iden[0:16, 0:16])],
             deps=[dD], prows=1, cols=16)
    dDf = read_bank(6, "vector", lambda E: E.tensor_copy(Dflat[:, 0:16], bk.sl(6, 1, 16)),
                    deps=[dz_df])
    dp1 = b.emit("scalar", lambda E: E.dma_start(out=cc_in.ap()[0:64, :], in_=KV_all[:]),
                 deps=dKV, inc="dCCp", dma=True)
    dp2 = b.emit("scalar", lambda E: E.dma_start(out=cc_in.ap()[64:65, :], in_=Dflat[:]),
                 deps=[dDf], inc="dCCp", dma=True)
    dcc = b.emit("gpsimd", lambda E:
                 E.collective_compute("AllGather", ALU.bypass,
                                      replica_groups=[[0, 1, 2, 3], [4, 5, 6, 7]],
                                      ins=[cc_in.ap().opt()], outs=[cc_out.ap().opt()]),
                 deps=[dp1, dp2], inc="cc")
    b.wait_only("gpsimd", [dcc])
    if "tap_kv" in tap_t:
        b.emit("sync", lambda E: E.dma_start(out=tap_t["tap_kv"].ap()[:], in_=KV_all[:]),
               deps=dKV, inc="dT_kv", dma=True)
    if upto < 6:
        return _ret()

    b.phase = "P4b"
    if p3sub == 4:
        b.wait_only("sync", [dWgh, dWoh])
        return _ret()

    dDs = b.emit("sync", lambda E: E.dma_start(
        out=r32A[:, 0:64],
        in_=cc_out.ap().rearrange("(r z) c -> r z c", z=65)[:, 64, 0:16]),
        deps=[dcc, dEBr], inc="dP4d", dma=True)
    mm_group(6, [lambda E, o, s0, s1:
                 E.matmul(o, onesall[32:33, 0:64], r32A[:, 0:64], start=s0, stop=s1,
                          **tp(32, 0))],
             deps=[dDs], prows=64, cols=64)
    dDr = read_bank(6, "vector", lambda E: E.tensor_copy(Drep[:], bk.sl(6, 64, 64)),
                    deps=[xn_stash[-1]])
    if p3sub == 5:
        return _ret()
    dk0 = b.emit("sync", lambda E: E.dma_start(out=B1[:], in_=cc_out.ap()[0:64, :]),
                 deps=[dcc] + wq_stage_dead, inc="dP4k", dma=True)
    dk1 = b.emit("sync", lambda E: E.dma_start(out=KVg[:], in_=cc_out.ap()[65:65 + 64, :]),
                 deps=[dcc], inc="dP4k", dma=True)
    dt_s0 = b.emit("vector", lambda E: E.tensor_scalar(B3[:], B1[:], selrep[:, 0:1], None,
                                                       ALU.mult),
                   deps=[dk0, dk1, cdep, xn_stash[-1]], inc="dv")
    dp2s = []
    for h in range(H):
        dp2s.append(b.emit("vector", lambda E, h=h:
                           E.scalar_tensor_tensor(B2[:, 64 * h:64 * (h + 1)],
                                                  B1[:, 64 * h:64 * (h + 1)],
                                                  Drep[:, 16 + h:16 + h + 1],
                                                  KVg[:, 64 * h:64 * (h + 1)],
                                                  ALU.mult, ALU.add),
                           deps=[dk0, dk1, dDr], inc="dv"))
    dt_s1 = b.emit("vector", lambda E: E.scalar_tensor_tensor(B1[:], B2[:], selrep[:, 1:2],
                                                              B3[:], ALU.mult, ALU.add),
                   deps=dp2s + [dt_s0], inc="dv")
    if p3sub == 6:
        return _ret()
    dk2 = b.emit("sync", lambda E: E.dma_start(out=KVg[:], in_=cc_out.ap()[130:130 + 64, :]),
                 deps=[dcc, dp2s[-1]], inc="dP4k", dma=True)
    dp3s = []
    for h in range(H):
        dp3s.append(b.emit("vector", lambda E, h=h:
                           E.scalar_tensor_tensor(B3[:, 64 * h:64 * (h + 1)],
                                                  B2[:, 64 * h:64 * (h + 1)],
                                                  Drep[:, 32 + h:32 + h + 1],
                                                  KVg[:, 64 * h:64 * (h + 1)],
                                                  ALU.mult, ALU.add),
                           deps=[dk2, dt_s1], inc="dv"))
    d_sinv = b.emit("vector", lambda E: E.scalar_tensor_tensor(B2[:], B3[:], selrep[:, 2:3],
                                                               B1[:], ALU.mult, ALU.add),
                    deps=dp3s, inc="dv")
    d_sin = b.emit("scalar", lambda E: E.dma_start(out=sinscr.ap()[:], in_=B2[:]),
                   deps=[d_sinv], inc="dSinSt", dma=True)
    if "tap_sin" in tap_t:
        b.emit("sync", lambda E: E.dma_start(out=tap_t["tap_sin"].ap()[:], in_=sinscr.ap()[:]),
               deps=[d_sin], inc="dT_sin", dma=True)
    if upto < 7:
        return _ret()

    b.phase = "P5"

    hp_war = [None, None]
    o_done = []
    sw2_war = [None]
    sa_war = [None]
    eps_stg_war = [None]
    dms = None
    for h in range(H):
        hp = head_prep(h, dep_extra=[d_sin])
        qrow = hp["qrow"]
        qcol = 2048 * (h // 2)
        dS0 = b.emit("sync", lambda E, h=h, qrow=qrow:
                     E.dma_start(out=Sbuf[0][qrow:qrow + 64, :],
                                 in_=sinscr.ap()[:, 64 * h:64 * (h + 1)]),
                     deps=[d_sin] + ([sw2_war[0]] if sw2_war[0] else []),
                     inc="dSin", dma=True)
        alld = hp["kdtm"] + hp["vtm"] + ([bk.rdep[7]] if bk.rdep[7] else [])
        pe_dep = None
        for c in range(NCH):
            for j in range(2):
                pe_dep = b.emit("tensor", lambda E, c=c, j=j, qrow=qrow:
                                E.matmul(bk.sl(7, 64, 64, 64 * c, qrow),
                                         kd_tm[:, 64 * (2 * c + j):64 * (2 * c + j + 1)],
                                         v_tm[:, 64 * (2 * c + j):64 * (2 * c + j + 1)],
                                         start=(j == 0), stop=(j == 1), **tp(0, qrow)),
                                deps=alld if (c == 0 and j == 0) else (),
                                inc="pe" if (c == NCH - 1 and j == 1) else None)
        bk.wdep[7] = pe_dep
        cur = 0
        dprev = dS0
        dSs = []
        for c in range(NCH):
            dts = b.emit("vector", lambda E, c=c, cur=cur, h=h, qrow=qrow:
                         E.tensor_scalar(Ss_all[qrow:qrow + 64, 64 * c:64 * (c + 1)],
                                         Sbuf[cur][qrow:qrow + 64, :],
                                         eBr_all[qrow:qrow + 64, 8 * h + c:8 * h + c + 1],
                                         None, ALU.mult),
                         deps=[dprev, dEBr], inc="dv")
            dSs.append(dts)
            dprev = b.emit("vector", lambda E, c=c, cur=cur, qrow=qrow:
                           E.tensor_tensor(Sbuf[1 - cur][qrow:qrow + 64, :],
                                           Ss_all[qrow:qrow + 64, 64 * c:64 * (c + 1)],
                                           bk.sl(7, 64, 64, 64 * c, qrow), ALU.add),
                           deps=[dts, pe_dep], inc="dv")
            cur = 1 - cur
        bk.rdep[7] = dprev
        oh_deps = []
        SA_pair = (SA_sb, SA_sbB)
        for t2 in range(4):
            for ci in range(2):
                c = 2 * t2 + ci
                for jh in range(2):
                    bi = 4 + 2 * ci + jh
                    mm_group(bi, [lambda E, o, s0, s1, c=c, jh=jh, qrow=qrow, qcol=qcol:
                                  E.matmul(o, kd_fm[qrow:qrow + 64,
                                                   256 * c + 128 * jh:256 * c + 128 * (jh + 1)],
                                           XQ[qrow:qrow + 64, qcol + 256 * c:qcol + 256 * (c + 1)],
                                           start=s0, stop=s1, **tp(qrow, 0))],
                             deps=hp["kdfm"] + [q_done[-1], tap_q_dep], cols=256)
            dsa_all = []
            for ci in range(2):
                sab = SA_pair[ci]
                for jh in range(2):
                    bi = 4 + 2 * ci + jh
                    dsa_all.append(read_bank(bi, "vector", lambda E, jh=jh, bi=bi, sab=sab:
                                             E.tensor_tensor(sab[:, 256 * jh:256 * (jh + 1)],
                                                             bk.sl(bi, 128, 256),
                                                             mask[:, 256 * jh:256 * (jh + 1)],
                                                             ALU.mult),
                                             deps=[cdep, xn_stash[-1]]
                                             + ([sa_war[0]] if sa_war[0] else [])))
            sa_war[0] = dsa_all[-1]
            for ci in range(2):
                c = 2 * t2 + ci
                sab = SA_pair[ci]
                bi = ci
                mms = [lambda E, o, s0, s1, c=c, qrow=qrow, sab=sab:
                       E.matmul(o, v_tm[:, 64 * 2 * c:64 * (2 * c + 1)],
                                sab[:, 0:256], start=s0, stop=s1, **tp(0, qrow)),
                       lambda E, o, s0, s1, c=c, qrow=qrow, sab=sab:
                       E.matmul(o, v_tm[:, 64 * (2 * c + 1):64 * (2 * c + 2)],
                                sab[:, 256:512], start=s0, stop=s1, **tp(0, qrow)),
                       lambda E, o, s0, s1, c=c, qrow=qrow, qcol=qcol:
                       E.matmul(o, Ss_all[qrow:qrow + 64, 64 * c:64 * (c + 1)],
                                XQ[qrow:qrow + 64, qcol + 256 * c:qcol + 256 * (c + 1)],
                                start=s0, stop=s1, **tp(qrow, qrow))]
                mm_group(bi, mms, deps=dsa_all + [dSs[c]], prows=64, cols=256, poff=qrow)
                oh_deps.append(read_bank(bi, "scalar", lambda E, c=c, bi=bi, qrow=qrow:
                                         E.activation(o_head[qrow:qrow + 64,
                                                             256 * c:256 * (c + 1)],
                                                      bk.sl(bi, 64, 256, poff=qrow), AF.Copy)))
        depst = b.emit("sync", lambda E, h=h:
                       E.dma_start(out=r_stB[:, :], in_=epsscr.ap()[h:h + 1, :]),
                       deps=[d_epsd] + ([eps_stg_war[0]] if eps_stg_war[0] else []),
                       inc="dEps", dma=True)
        dms_list = []
        for n in range(4):
            do2 = b.emit("vector", lambda E, n=n, qrow=qrow:
                         E.tensor_tensor(o2scr[qrow:qrow + 64, :],
                                         o_head[qrow:qrow + 64, 512 * n:512 * (n + 1)],
                                         o_head[qrow:qrow + 64, 512 * n:512 * (n + 1)],
                                         ALU.mult),
                         deps=oh_deps + dms_list[-1:], inc="dv")
            bi = 2 + (n % 2)
            mm_group(bi, [lambda E, o, s0, s1, qrow=qrow:
                          E.matmul(o, o64div[qrow:qrow + 64, :], o2scr[qrow:qrow + 64, :],
                                   start=s0, stop=s1, **tp(qrow, 0))],
                     deps=[do2], prows=1)
            dms_list.append(read_bank(bi, "vector", lambda E, n=n, bi=bi:
                                      E.tensor_tensor(r_stC[:, 512 * n:512 * (n + 1)],
                                                      bk.sl(bi, 1),
                                                      r_stB[:, 512 * n:512 * (n + 1)],
                                                      ALU.add),
                                      deps=[depst, d_mp]))
        dms = dms_list[-1]
        eps_stg_war[0] = dms
        dsq_ = b.emit("scalar", lambda E: E.activation(r32A[:, :], r_stC[:, :], AF.Sqrt),
                      deps=[dms, dDr], inc="sc")
        b.emit("vector", lambda E: E.drain(), deps=[dsq_])
        drc = b.emit("vector", lambda E: E.reciprocal(r32B[:, :], r32A[:, :]),
                     deps=[dsq_], inc="dv")
        for n in range(4):
            bi = 2 + (n % 2)
            mm_group(bi, [lambda E, o, s0, s1, n=n, qrow=qrow:
                          E.matmul(o, onesall[32:33, 0:64], r32B[:, 512 * n:512 * (n + 1)],
                                   start=s0, stop=s1, **tp(32, qrow))],
                     deps=[drc], prows=64, poff=qrow)
            o_done.append(read_bank(bi, "vector",
                                    lambda E, n=n, bi=bi, qrow=qrow, qcol=qcol:
                                    E.tensor_tensor(XQ[qrow:qrow + 64,
                                                       qcol + 512 * n:qcol + 512 * (n + 1)],
                                                    o_head[qrow:qrow + 64, 512 * n:512 * (n + 1)],
                                                    bk.sl(bi, 64, 512, poff=qrow), ALU.mult),
                                    deps=[tap_q_dep]))
        sw2_war[0] = dprev
        hp_war[h % 2] = [hp["vtm"][-1], hp["kdfm"][-1]]
        kd_war[0] = oh_deps[-1]
        vtm_war[0] = oh_deps[-1]
    if "tap_o" in tap_t:
        b.emit("sync", lambda E: E.dma_start(out=tap_t["tap_o"].ap()[:], in_=XQ[:]),
               deps=o_done, inc="dT_o", dma=True)
    if upto < 8:
        return _ret()

    b.phase = "P6"

    p5_fin = [b.now("pe"), b.now("dv"), b.now("sc"), b.now("gp")]
    xq_hi = at("p6_xqh", [128, 4096], BF16, SCR0)
    xq_lo = at("p6_xql", [128, 4096], BF16, SCR0 + 8192)
    PHs = at("p6_ph", [128, 4096], BF16, SCR0 + 16384)
    PLs = at("p6_pl", [128, 4096], BF16, SCR0 + 24576)
    xtile = [at(f"p6_xt{i}", [128, 1024], F32, SCR0 + 32768 + 4096 * i) for i in range(2)]
    t1s = at("p6_t1", [128, 512], F32, SCR0 + 40960)
    pps = at("p6_pp", [128, 512], F32, SCR0 + 43008)
    outsb = [at(f"p6_ot{i}", [128, 512], F32, SCR0 + 45056 + 2048 * i) for i in range(2)]
    wstC2 = [at(f"p6_wst{i}", [128, 512], F32, 12288 + 2048 * 4 + 2048 * i)
             for i in range(2)]
    wstC_war = [None, None]
    dWg = load_wlo_only("wg", WGg_hi, WGg_lo, wstC2, wstC_war,
                        deps=[dWgh, d_sinv] + p5_fin)
    dWo = load_wlo_only("wo", WGo_hi, WGo_lo, wstC2, wstC_war,
                        deps=[dWoh, dWg] + p5_fin)
    acc_done = []
    xnq_war = [None]
    ph_war = [None]
    ot_war = [None, None]
    xt_war = [None, None]
    for n in range(NQ):
        dxh = b.emit("sync", lambda E, n=n: E.dma_start(out=xq_hi[:], in_=xnscr_hi.ap()[n]),
                     deps=xn_stash[-1:] + ([xnq_war[0]] if xnq_war[0] else p5_fin)
                     + b.tapdeps, inc="dXq6", dma=True)
        dxl = b.emit("sync", lambda E, n=n: E.dma_start(out=xq_lo[:], in_=xnscr_lo.ap()[n]),
                     deps=xn_stash[-1:] + ([xnq_war[0]] if xnq_war[0] else p5_fin),
                     inc="dXq6", dma=True)
        p_done = []
        t1_war = [None]
        pp_war = [None]
        for m in range(8):
            bi = m % 4
            mms = []
            for k in range(8):
                for wt, xt in ((WGg_hi, xq_hi), (WGg_hi, xq_lo), (WGg_lo, xq_hi)):
                    mms.append(lambda E, o, s0, s1, k=k, m=m, wt=wt, xt=xt:
                               E.matmul(o, wt[:, 1024 * k + 128 * m:1024 * k + 128 * (m + 1)],
                                        xt[:, 512 * k:512 * (k + 1)], start=s0, stop=s1))
            mm_group(bi, mms, deps=[dWg, dxh, dxl, o_done[-1]])
            dt1 = read_bank(bi, "scalar", lambda E, bi=bi:
                            E.activation(t1s[:], bk.sl(bi), AF.Silu),
                            deps=[pp_war[0]] if pp_war[0] else [])
            dpp = b.emit("vector", lambda E, m=m, n=n:
                         E.tensor_tensor(pps[:], t1s[:],
                                         XQ[:, 2048 * m + 512 * n:2048 * m + 512 * (n + 1)],
                                         ALU.mult),
                         deps=[dt1, o_done[-1]] + ([t1_war[0]] if t1_war[0] else []), inc="dv")
            dph = b.emit("scalar", lambda E, m=m:
                         E.activation(PHs[:, 512 * m:512 * (m + 1)], pps[:], AF.Copy),
                         deps=[dpp] + ([ph_war[0]] if ph_war[0] else []), inc="sc")
            dpl = b.emit("vector", lambda E, m=m:
                         E.tensor_tensor(PLs[:, 512 * m:512 * (m + 1)], pps[:],
                                         PHs[:, 512 * m:512 * (m + 1)], ALU.subtract),
                         deps=[dph], inc="dv")
            t1_war[0] = dpp
            pp_war[0] = dpl
            p_done += [dph, dpl]
        for tt in range(4):
            gtt = 4 * n + tt
            dxt = b.emit("sync", lambda E, gtt=gtt, tt=tt:
                         E.dma_start(out=xtile[tt % 2][:],
                                     in_=din["x"].ap()[128 * gtt:128 * (gtt + 1), :]),
                         deps=([xt_war[tt % 2]] if xt_war[tt % 2] else p5_fin) + b.tapdeps,
                         inc="dXt%d" % (tt % 2), dma=True)
            for oc in range(2):
                bi = 4 + 2 * (tt % 2) + oc
                mms = []
                for k in range(8):
                    for pt, wt in ((PHs, WGo_hi), (PLs, WGo_hi), (PHs, WGo_lo)):
                        mms.append(lambda E, o, s0, s1, k=k, tt=tt, oc=oc, pt=pt, wt=wt:
                                   E.matmul(o, pt[:, 512 * k + 128 * tt:512 * k + 128 * (tt + 1)],
                                            wt[:, 1024 * k + 512 * oc:1024 * k + 512 * (oc + 1)],
                                            start=s0, stop=s1))
                mm_group(bi, mms, deps=p_done + [dWo])
                dev = read_bank(bi, "vector", lambda E, tt=tt, oc=oc, bi=bi:
                                E.tensor_tensor(outsb[tt % 2][:], bk.sl(bi),
                                                xtile[tt % 2][:, 512 * oc:512 * (oc + 1)],
                                                ALU.add),
                                deps=[dxt] + ([ot_war[tt % 2]] if ot_war[tt % 2] else []))
                dst = b.emit("scalar", lambda E, gtt=gtt, oc=oc, tt=tt:
                             E.dma_start(out=accscr.ap()[128 * gtt:128 * (gtt + 1),
                                                         512 * oc:512 * (oc + 1)],
                                         in_=outsb[tt % 2][:]),
                             deps=[dev], inc="dAcc", dma=True)
                ot_war[tt % 2] = dst
                acc_done.append(dst)
            xt_war[tt % 2] = b.now("dv")
        xnq_war[0] = b.now("pe")
        ph_war[0] = b.now("pe")
    if "tap_acc" in tap_t:
        b.emit("sync", lambda E: E.dma_start(out=tap_t["tap_acc"].ap()[:], in_=accscr.ap()[:]),
               deps=[acc_done[-1]], inc="dT_acc", dma=True)
    if upto < 9:
        return _ret()

    b.phase = "P7"

    p6_fin = [b.now("pe"), b.now("dv"), b.now("sc"), b.now("gp"), acc_done[-1]]
    xnq_hi = at("p7_xh", [128, 4096], BF16, SCR0)
    xnq_lo = at("p7_xl", [128, 4096], BF16, SCR0 + 8192)
    wp1 = [at(f"p7_w1{i}", [128, 1024], BF16, SCR0 + 16384 + 2048 * i) for i in range(2)]
    wp2 = [at(f"p7_w2{i}", [128, 1024], BF16, SCR0 + 20480 + 2048 * i) for i in range(2)]
    hpan = [at(f"p7_hp{i}", [128, 4096], BF16, SCR0 + 24576 + 8192 * i) for i in range(2)]
    w3p = [at(f"p7_w3{i}", [128, 1024], BF16, SCR0 + 40960 + 2048 * i) for i in range(2)]
    jscr = at("p7_js", [128, 512], F32, SCR0 + 45056)
    jp = at("p7_jp", [128, 512], F32, SCR0 + 47104)
    acct = at("p7_acc", [128, 1024], F32, 12288)
    outt = [at(f"p7_o{i}", [128, 512], F32, 12288 + 4096 + 2048 * i) for i in range(2)]
    s3rep = rowpack[:, 1024:2048]

    ds3s = b.emit("sync", lambda E: E.dma_start(out=rowpack[0:1, 0:1024], in_=din["s3r"].ap()[:]),
                  deps=p6_fin + [d_sin, stg_war[0]], inc="dS3", dma=True)
    s3_deps = []
    for oc in range(2):
        bi = oc
        mm_group(bi, [lambda E, o, s0, s1, oc=oc:
                      E.matmul(o, onesall[0:1, 0:128], rowpack[0:1, 512 * oc:512 * (oc + 1)],
                               start=s0, stop=s1)], deps=[ds3s] + acc_done[-2:])
        s3_deps.append(read_bank(bi, "vector", lambda E, oc=oc, bi=bi:
                                 E.tensor_copy(s3rep[:, 512 * oc:512 * (oc + 1)], bk.sl(bi)),
                                 deps=p6_fin))

    HIN = (HA_hi, HA_lo)
    HOUT = (XN_hi, XN_lo)
    qglob_js_war = [acc_done[-1]]
    qglob_jp_war = [acc_done[-1]]
    out_final = []
    pwar = {"w1": [None, None], "w2": [None, None], "hp": [None, None], "w3": [None, None]}
    hwr = {"A": list(p6_fin), "B": list(p6_fin)}
    xq7_war = [None]
    acct_war = [None]
    out_war = [None, None]

    for q in range(NQ):
        dxh = b.emit("sync", lambda E, q=q: E.dma_start(out=xnq_hi[:], in_=xnscr_hi.ap()[q]),
                     deps=xn_stash[-1:] + ([xq7_war[0]] if xq7_war[0] else p6_fin)
                     + b.tapdeps, inc="dXq7", dma=True)
        dxl = b.emit("sync", lambda E, q=q: E.dma_start(out=xnq_lo[:], in_=xnscr_lo.ap()[q]),
                     deps=xn_stash[-1:] + ([xq7_war[0]] if xq7_war[0] else p6_fin),
                     inc="dXq7", dma=True)
        hin_done = []
        js_war = [qglob_js_war[0]]
        jp_war = [qglob_jp_war[0]]
        for m in range(32):
            pb1, pb2 = wp1[m % 2], wp2[m % 2]
            dw1 = b.emit("gpsimd", lambda E, m=m, pb1=pb1:
                         E.dma_start(out=pb1[:], in_=din["w1b"].ap()[m]),
                         deps=([pwar["w1"][m % 2]] if pwar["w1"][m % 2] else p6_fin),
                         inc="dPw1%d" % (m % 2), dma=True)
            dw2 = b.emit("gpsimd", lambda E, m=m, pb2=pb2:
                         E.dma_start(out=pb2[:], in_=din["w2b"].ap()[m]),
                         deps=([pwar["w2"][m % 2]] if pwar["w2"][m % 2] else p6_fin),
                         inc="dPw2%d" % (m % 2), dma=True)
            bi1, bi2 = m % 2, 2 + m % 2
            mms1, mms2 = [], []
            for k in range(8):
                for xt in (xnq_hi, xnq_lo):
                    mms1.append(lambda E, o, s0, s1, k=k, pb1=pb1, xt=xt:
                                E.matmul(o, pb1[:, 128 * k:128 * (k + 1)],
                                         xt[:, 512 * k:512 * (k + 1)], start=s0, stop=s1))
                    mms2.append(lambda E, o, s0, s1, k=k, pb2=pb2, xt=xt:
                                E.matmul(o, pb2[:, 128 * k:128 * (k + 1)],
                                         xt[:, 512 * k:512 * (k + 1)], start=s0, stop=s1))
            d1_ = mm_group(bi1, mms1, deps=[dw1, dxh, dxl])
            d2_ = mm_group(bi2, mms2, deps=[dw2, dxh, dxl])
            pwar["w1"][m % 2] = d1_
            pwar["w2"][m % 2] = d2_
            dt1 = read_bank(bi1, "scalar", lambda E, m=m, bi1=bi1:
                            E.activation(jscr[:], bk.sl(bi1), AF.Silu, scale=s1h[:, m:m + 1]),
                            deps=[js_war[0]] if js_war[0] else [])
            dpp = read_bank(bi2, "vector", lambda E, m=m, bi2=bi2:
                            E.scalar_tensor_tensor(jp[:], bk.sl(bi2), s2h[:, m:m + 1],
                                                   jscr[:], ALU.mult, ALU.mult),
                            deps=[dt1] + ([jp_war[0]] if jp_war[0] else []))
            dh = b.emit("scalar", lambda E, m=m:
                        E.activation(HIN[0][:, 512 * m:512 * (m + 1)], jp[:], AF.Copy),
                        deps=[dpp] + [d for d in hwr["A"] if d], inc="sc")
            dl = b.emit("vector", lambda E, m=m:
                        E.tensor_tensor(HIN[1][:, 512 * m:512 * (m + 1)], jp[:],
                                        HIN[0][:, 512 * m:512 * (m + 1)], ALU.subtract),
                        deps=[dh], inc="dv")
            js_war[0] = dpp
            jp_war[0] = dl
            hin_done += [dh, dl]
        hwr["A"] = [None]
        for stage, wname, sc_ap, src, dst, dwkey in (
                ("B", "h1b", sh1h, HIN, HOUT, "B"),
                ("C", "h2b", sh2h, HOUT, HIN, "A")):
            sdone = []
            hjs_war = [None]
            for m in range(32):
                pb = hpan[m % 2]
                dw = b.emit("gpsimd", lambda E, m=m, pb=pb, wname=wname:
                            E.dma_start(out=pb[:], in_=din[wname].ap()[m]),
                            deps=([pwar["hp"][m % 2]] if pwar["hp"][m % 2] else p6_fin),
                            inc="dPh%d" % (m % 2), dma=True)
                bi = m % 2
                mms = []
                for k in range(32):
                    for xt in (src[0], src[1]):
                        mms.append(lambda E, o, s0, s1, k=k, pb=pb, xt=xt:
                                   E.matmul(o, pb[:, 128 * k:128 * (k + 1)],
                                            xt[:, 512 * k:512 * (k + 1)], start=s0, stop=s1))
                dmm = mm_group(bi, mms, deps=([dw] + hin_done) if m < 2 else [dw])
                pwar["hp"][m % 2] = dmm
                dt = read_bank(bi, "scalar", lambda E, m=m, bi=bi, sc_ap=sc_ap:
                               E.activation(jscr[:], bk.sl(bi), AF.Gelu, scale=sc_ap[:, m:m + 1]),
                               deps=[hjs_war[0]] if hjs_war[0] else [js_war[0]])
                dh = b.emit("scalar", lambda E, m=m, dst=dst:
                            E.activation(dst[0][:, 512 * m:512 * (m + 1)], jscr[:], AF.Copy),
                            deps=[dt] + [d for d in hwr[dwkey] if d], inc="sc")
                dl = b.emit("vector", lambda E, m=m, dst=dst:
                            E.tensor_tensor(dst[1][:, 512 * m:512 * (m + 1)], jscr[:],
                                            dst[0][:, 512 * m:512 * (m + 1)], ALU.subtract),
                            deps=[dh, jp_war[0]], inc="dv")
                hjs_war[0] = dl
                sdone += [dh, dl]
            hwr[dwkey] = [None]
            hin_done = sdone
        w3jp_war = [jp_war[0]]
        for k in range(32):
            pb = w3p[k % 2]
            dw = b.emit("gpsimd", lambda E, k=k, pb=pb:
                        E.dma_start(out=pb[:], in_=din["w3b"].ap()[k]),
                        deps=([pwar["w3"][k % 2]] if pwar["w3"][k % 2] else p6_fin),
                        inc="dPw3%d" % (k % 2), dma=True)
            last = None
            for tt in range(4):
                for oc in range(2):
                    bi = 2 * tt + oc
                    outap = bk.sl(bi)
                    alldeps = [dw] + (hin_done if k == 0 else [])
                    if k == 0 and bk.rdep[bi] is not None:
                        alldeps.append(bk.rdep[bi])
                    for pi, pt in enumerate((HIN[0], HIN[1])):
                        first = (k == 0 and pi == 0)
                        firstk = (tt == 0 and oc == 0 and pi == 0)
                        lastf = (k == 31 and pi == 1)
                        lastk = (tt == 3 and oc == 1 and pi == 1)
                        last = b.emit("tensor",
                                      lambda E, o=outap, k=k, tt=tt, oc=oc, pt=pt, pb=pb,
                                      first=first, lastf=lastf:
                                      E.matmul(o, pt[:, 512 * k + 128 * tt:512 * k + 128 * (tt + 1)],
                                               pb[:, 512 * oc:512 * (oc + 1)],
                                               start=first, stop=lastf),
                                      deps=alldeps if firstk else (),
                                      inc="pe" if (lastf or lastk) else None)
                    if k == 31:
                        bk.wdep[bi] = last
            pwar["w3"][k % 2] = last
        for tt in range(4):
            gtt = 4 * q + tt
            dac = b.emit("sync", lambda E, gtt=gtt:
                         E.dma_start(out=acct[:], in_=accscr.ap()[128 * gtt:128 * (gtt + 1), :]),
                         deps=[acc_done[-1]] + ([acct_war[0]] if acct_war[0] else p6_fin),
                         inc="dAcct", dma=True)
            for oc in range(2):
                bi = 2 * tt + oc
                dt = read_bank(bi, "vector", lambda E, bi=bi, oc=oc:
                               E.tensor_tensor(jp[:], bk.sl(bi), s3rep[:, 512 * oc:512 * (oc + 1)],
                                               ALU.mult),
                               deps=s3_deps + ([out_war[oc]] if out_war[oc] else [])
                               + [w3jp_war[0]])
                dt2 = b.emit("vector", lambda E, oc=oc:
                             E.tensor_tensor(outt[oc][:], jp[:], acct[:, 512 * oc:512 * (oc + 1)],
                                             ALU.add), deps=[dt, dac], inc="dv")
                w3jp_war[0] = dt2
                dst = b.emit("scalar", lambda E, gtt=gtt, oc=oc:
                             E.dma_start(out=out.ap()[128 * gtt:128 * (gtt + 1),
                                                      512 * oc:512 * (oc + 1)],
                                         in_=outt[oc][:]),
                             deps=[dt2], inc="dOut%d" % oc, dma=True)
                out_war[oc] = dst
                out_final.append(dst)
            acct_war[0] = b.now("dv")
        xq7_war[0] = b.now("pe")
        hwr["A"] = [b.now("pe")]
        hwr["B"] = [b.now("pe")]
        qglob_js_war[0] = b.now("sc")
        qglob_jp_war[0] = b.now("dv")

    b.wait_only("sync", [out_final[-1], out_final[-2]] + b.tapdeps)
    return _finish(nc, ctx, b)


def _finish(nc, ctx, b):
    def _replay(E, eng):
        for item in b.plan:
            if item[0] == eng:
                with nc.named_scope(item[2]):
                    item[1](E)

    with nc.Block() as block:
        @block.tensor
        def _(E):
            _replay(E, "tensor")

        @block.vector
        def _(E):
            _replay(E, "vector")

        @block.scalar
        def _(E):
            _replay(E, "scalar")

        @block.gpsimd
        def _(E):
            _replay(E, "gpsimd")

        @block.sync
        def _(E):
            _replay(E, "sync")
    ctx.close()
    return nc


def kernel(**inputs):
    in_maps = host_prep(inputs)
    nc = build()
    res = run_bass_kernel_spmd(nc, in_maps, core_ids=list(range(NCORES)))
    outs = [res.results[c]["out"] for c in range(NCORES)]
    return np.concatenate(outs, axis=0).reshape(2, 8192, D).astype(np.float32)


# version 07
# speedup vs baseline: 1.0748x; 1.0748x over previous
import numpy as np
import ml_dtypes
import concourse.bass as bass
import concourse.mybir as mybir
from concourse.bass_utils import run_bass_kernel_spmd
from contextlib import ExitStack

F32 = mybir.dt.float32
BF16 = mybir.dt.bfloat16
U8 = mybir.dt.uint8
AF = mybir.ActivationFunctionType
ALU = mybir.AluOpType
AX = mybir.AxisListType
BF = ml_dtypes.bfloat16

NCORES = 8
T = 2048
NTT = 16
D = 1024
H = 16
DH = 64
CH = 256
NCH = 8
GN = 16.0
EPS_RMS = 1e-6
EPS_LN = 1e-5
NQ = 4
QT = 512

ENGINES = ("tensor", "vector", "scalar", "gpsimd", "sync")


def tp(r, c):
    return {} if (r == 0 and c == 0) else {"tile_position": (r, c)}


class Builder:
    def __init__(self, nc, es):
        self.nc = nc
        self.plan = []
        self.semh = {}
        self.semv = {}
        self.lastwait = {e: {} for e in ENGINES}
        self.es = es
        self.tapdeps = []
        self.phase = "P0"

    def sem(self, name):
        self.semh[name] = self.es.enter_context(self.nc.semaphore(name))
        self.semv[name] = 0

    def _waits(self, eng, deps):
        best = {}
        for d in deps:
            if d is None:
                continue
            s, v = d
            if v <= 0:
                continue
            best[s] = max(best.get(s, 0), v)
        out = []
        for s, v in best.items():
            if self.lastwait[eng].get(s, 0) < v:
                self.lastwait[eng][s] = v
                out.append((self.semh[s], v))
        return out

    def emit(self, eng, make, deps=(), inc=None, dma=False):
        waits = self._waits(eng, deps)
        step = 16 if dma else 1
        tgt = None
        semh = None
        if inc is not None:
            if inc not in self.semh:
                self.sem(inc)
            self.semv[inc] += step
            tgt = (inc, self.semv[inc])
            semh = self.semh[inc]

        def fn(E, waits=waits, make=make, semh=semh, step=step):
            for (h, v) in waits:
                E.wait_ge(h, v)
            ins = make(E)
            if semh is not None:
                ins.then_inc(semh, step)
        self.plan.append((eng, fn, self.phase))
        if inc is not None and inc.startswith("dT_"):
            self.tapdeps.append(tgt)
        return tgt

    def wait_only(self, eng, deps):
        waits = self._waits(eng, deps)
        if not waits:
            return

        def fn(E, waits=waits):
            for (h, v) in waits:
                E.wait_ge(h, v)
        self.plan.append((eng, fn, self.phase))

    def now(self, sem):
        return (sem, self.semv.get(sem, 0))


class Banks:
    def __init__(self, ps):
        self.ps = ps
        self.wdep = [None] * 8
        self.rdep = [None] * 8

    def sl(self, i, prows=128, cols=512, coff=0, poff=0):
        t = self.ps[i // 4]
        c0 = (i % 4) * 512 + coff
        return t[poff:poff + prows, c0:c0 + cols]


def host_prep(inputs):
    x = np.ascontiguousarray(np.asarray(inputs["L0_attn_x"], np.float32).reshape(16384, D))
    ws = {}
    for k in ("q", "k", "v", "g", "gt", "o"):
        ws["w" + k] = np.ascontiguousarray(np.asarray(inputs["L0_attn_W" + k], np.float32))

    def blockT(w, mch, kch):
        return np.ascontiguousarray(w.reshape(mch, 128, kch, 128).transpose(0, 3, 2, 1))

    w1b = blockT(np.asarray(inputs["L0_ffn_w1_w"], np.float32), 32, 8)
    w2b = blockT(np.asarray(inputs["L0_ffn_w2_w"], np.float32), 32, 8)
    h1b = blockT(np.asarray(inputs["L0_ffn_h1_w"], np.float32), 32, 32)
    h2b = blockT(np.asarray(inputs["L0_ffn_h2_w"], np.float32), 32, 32)
    w3b = np.ascontiguousarray(np.asarray(inputs["L0_ffn_w3_w"], np.float32).T.reshape(32, 128, 1024))
    s1h = np.ascontiguousarray(np.asarray(inputs["L0_ffn_w1_s"], np.float32).reshape(32, 128).T)
    s2h = np.ascontiguousarray(np.asarray(inputs["L0_ffn_w2_s"], np.float32).reshape(32, 128).T)
    sh1h = np.ascontiguousarray(np.asarray(inputs["L0_ffn_h1_s"], np.float32).reshape(32, 128).T)
    sh2h = np.ascontiguousarray(np.asarray(inputs["L0_ffn_h2_s"], np.float32).reshape(32, 128).T)
    s3r = np.ascontiguousarray(np.asarray(inputs["L0_ffn_w3_s"], np.float32).reshape(1, 1024))

    iden = np.eye(128, dtype=np.float32)
    mask = np.zeros((128, 512), np.float32)
    jj = np.arange(128)[:, None]
    ii = np.arange(256)[None, :]
    mask[:, 0:256] = (jj <= ii).astype(np.float32)
    mask[:, 256:512] = ((jj + 128) <= ii).astype(np.float32)

    in_maps = []
    for c in range(NCORES):
        sel = np.zeros((64, 3), np.float32)
        pos = c % 4
        if pos >= 1:
            sel[:, pos - 1] = 1.0
        m = {"x": x[T * c:T * (c + 1)],
             "w1b": w1b, "w2b": w2b, "h1b": h1b, "h2b": h2b, "w3b": w3b,
             "s1h": s1h, "s2h": s2h, "sh1h": sh1h, "sh2h": sh2h, "s3r": s3r,
             "iden": iden, "mask": mask,
             "onesall": np.ones((128, 128), np.float32),
             "o64div": np.full((128, 1), 1.0 / DH, np.float32),
             "fconst": np.repeat(np.array([[EPS_LN, EPS_RMS, 0.0, 1.0 / D,
                                            1.0 / GN, 2.0 / GN, 1.0 / DH, 64.0 * EPS_RMS]],
                                          np.float32), 128, axis=0),
             "selrep": sel}
        m.update(ws)
        in_maps.append(m)
    return in_maps


def build(taps=(), upto=99, p3sub=0):
    nc = bass.Bass("TRN2", target_bir_lowering=False)
    ctx = ExitStack()

    din = {}
    for name, shape in [
            ("x", [T, D]), ("wq", [D, D]), ("wk", [D, D]), ("wv", [D, D]),
            ("wg", [D, D]), ("wgt", [D, 16]), ("wo", [D, D]),
            ("w1b", [32, 128, 8, 128]), ("w2b", [32, 128, 8, 128]),
            ("h1b", [32, 128, 32, 128]), ("h2b", [32, 128, 32, 128]),
            ("w3b", [32, 128, 1024]),
            ("s1h", [128, 32]), ("s2h", [128, 32]), ("sh1h", [128, 32]),
            ("sh2h", [128, 32]), ("s3r", [1, 1024]),
            ("iden", [128, 128]), ("mask", [128, 512]),
            ("onesall", [128, 128]), ("o64div", [128, 1]), ("selrep", [64, 3]),
            ("fconst", [128, 8])]:
        din[name] = nc.dram_tensor(name, shape, F32, kind="ExternalInput")
    out = nc.dram_tensor("out", [T, D], F32, kind="ExternalOutput")

    xnscr_hi = nc.dram_tensor("xnscr_hi", [NQ, 128, 8 * QT], BF16, kind="Internal")
    xnscr_lo = nc.dram_tensor("xnscr_lo", [NQ, 128, 8 * QT], BF16, kind="Internal")
    kscr = nc.dram_tensor("kscr", [D, T], F32, kind="Internal")
    vscr = nc.dram_tensor("vscr", [D, T], F32, kind="Internal")
    fkscr = nc.dram_tensor("fkscr", [16, T], F32, kind="Internal")
    epsscr = nc.dram_tensor("epsscr", [16, T], F32, kind="Internal")
    accscr = nc.dram_tensor("accscr", [T, D], F32, kind="Internal")
    sinscr = nc.dram_tensor("sinscr", [64, 1024], F32, kind="Internal")
    cc_in = nc.dram_tensor("cc_in", [65, 1024], F32, kind="Internal")
    cc_out = nc.dram_tensor("cc_out", [260, 1024], F32, kind="Internal")
    ebscr = nc.dram_tensor("ebscr", [1, 128], F32, kind="Internal")

    tap_t = {}
    for tname, shape, dt in [
            ("tap_xnhi", [128, 16384], BF16), ("tap_xnlo", [128, 16384], BF16),
            ("tap_gt", [16, 2 * T], F32),
            ("tap_k", [D, T], F32), ("tap_v", [D, T], F32),
            ("tap_fk", [16, T], F32), ("tap_eps", [16, T], F32),
            ("tap_q", [128, 16384], F32), ("tap_kv", [64, 1024], F32),
            ("tap_sin", [64, 1024], F32), ("tap_o", [128, 16384], F32),
            ("tap_acc", [T, D], F32)]:
        if tname in taps:
            tap_t[tname] = nc.dram_tensor(tname, shape, dt, kind="ExternalOutput")

    ARENA_B = 204 * 1024
    arena_h = ctx.enter_context(nc.sbuf_tensor("arena", [128, ARENA_B], U8))
    base = nc.lookup_mloc(arena_h).addr

    def at(name, shape, dtype, off):
        per = int(np.prod(shape[1:])) * mybir.dt.size(dtype)
        assert off + per <= ARENA_B, f"{name}: {off}+{per} > {ARENA_B}"
        h = nc.virtual_sbuf_tensor(name, shape, dtype)
        nc._tensor(h.name, h.shape, h.dtype, type="SB")
        mloc = nc.lookup_mloc(h)
        mloc.allocated = True
        mloc.addr = base + off
        return h

    iden = at("c_iden", [128, 128], F32, 0)
    mask = at("c_mask", [128, 512], F32, 512)
    onesall = at("c_onesall", [128, 128], F32, 2560)
    o64div = at("c_o64d", [128, 1], F32, 3072)
    selrep = at("c_sel", [64, 3], F32, 3104)
    s1h = at("c_s1h", [128, 32], F32, 3136)
    s2h = at("c_s2h", [128, 32], F32, 3264)
    sh1h = at("c_sh1h", [128, 32], F32, 3392)
    sh2h = at("c_sh2h", [128, 32], F32, 3520)
    fconst = at("c_fconst", [128, 8], F32, 3648)
    expB = at("c_expB", [16, 8], F32, 3680)
    Drow = at("c_drow", [16, 1], F32, 3712)
    Drow2 = at("c_drow2", [16, 1], F32, 3744)
    for _i, _v in enumerate([EPS_LN, EPS_RMS, 0.0, 1.0 / D, 1.0 / GN, 2.0 / GN,
                             1.0 / DH, 64.0 * EPS_RMS]):
        if _v != 0.0:
            nc.const_aps.register(F32, float(_v), fconst[:, _i:_i + 1])
    rowpack = at("rowpack", [128, 2048], F32, 4096)
    rowpack2 = at("rowpack2", [128, 4096], F32, 12288)
    r_stA = rowpack[0:1, :]
    r_stB = rowpack2[0:1, 0:2048]
    r_stC = rowpack2[0:1, 2048:4096]
    r32A = rowpack[32:33, :]
    r32B = rowpack2[32:33, 0:2048]

    XN_hi = at("XN_hi", [128, 16384], BF16, 28672)
    XN_lo = at("XN_lo", [128, 16384], BF16, 61440)
    XQ = at("XQ", [128, 16384], F32, 94208)
    SCR0 = 159744
    SCRB = ARENA_B - SCR0
    assert SCRB == 49152

    WSa_hi = at("WSa_hi", [128, 8192], BF16, 94208)
    WSa_lo = at("WSa_lo", [128, 8192], BF16, 94208 + 16384)
    WSb_hi = at("WSb_hi", [128, 8192], BF16, 94208 + 32768)
    WSb_lo = at("WSb_lo", [128, 8192], BF16, 94208 + 49152)
    HA_hi = at("HA_hi", [128, 16384], BF16, 94208)
    HA_lo = at("HA_lo", [128, 16384], BF16, 94208 + 32768)
    WGg_hi = at("WGg_hi", [128, 8192], BF16, 28672)
    WGg_lo = at("WGg_lo", [128, 8192], BF16, 28672 + 16384)
    WGo_hi = at("WGo_hi", [128, 8192], BF16, 61440)
    WGo_lo = at("WGo_lo", [128, 8192], BF16, 61440 + 16384)
    B2 = at("p4_b2", [64, 1024], F32, 28672 + 16384)
    B3 = at("p4_b3", [64, 1024], F32, 28672 + 20480)
    Drep = at("p4_drep", [64, 64], F32, 28672 + 24576)
    SA_sb = at("p5_sa", [128, 512], F32, 61440 + 16384)
    o2scr = at("p5_o2", [128, 512], F32, 61440 + 18432)
    SA_sbB = at("p5_sab", [128, 512], F32, 61440 + 20480)

    ps = [ctx.enter_context(nc.psum_tensor(f"PS{i}", [128, 2048], F32)) for i in range(2)]
    bk = Banks(ps)
    b = Builder(nc, ctx)
    for s in ("pe", "dv", "sc", "gp", "cc"):
        b.sem(s)

    def mm_group(bank_i, mms, deps=(), prows=128, cols=512, coff=0, poff=0):
        outap = bk.sl(bank_i, prows, cols, coff, poff)
        alldeps = list(deps)
        if bk.rdep[bank_i] is not None:
            alldeps.append(bk.rdep[bank_i])
        n = len(mms)
        dep = None
        for i, m in enumerate(mms):
            dep = b.emit("tensor",
                         (lambda E, m=m, i=i, n=n, outap=outap: m(E, outap, i == 0, i == n - 1)),
                         deps=alldeps if i == 0 else (),
                         inc="pe" if i == n - 1 else None)
        bk.wdep[bank_i] = dep
        return dep

    def read_bank(bank_i, eng, make, deps=()):
        inc = {"vector": "dv", "scalar": "sc", "gpsimd": "gp"}[eng]
        dep = b.emit(eng, make, deps=list(deps) + [bk.wdep[bank_i]], inc=inc)
        bk.rdep[bank_i] = dep
        return dep

    def _ret():
        b.wait_only("sync", b.tapdeps
                    + [b.now("pe"), b.now("dv"), b.now("sc"), b.now("gp"), b.now("cc")])
        return _finish(nc, ctx, b)

    cdep = None
    for name, hdl in [("iden", iden), ("mask", mask), ("onesall", onesall),
                      ("o64div", o64div), ("selrep", selrep), ("s1h", s1h),
                      ("s2h", s2h), ("sh1h", sh1h), ("sh2h", sh2h), ("fconst", fconst)]:
        cdep = b.emit("sync", lambda E, d=din[name], h=hdl:
                      E.dma_start(out=h[:], in_=d.ap()[:]), inc="dCst", dma=True)
    for _e in ("tensor", "vector", "scalar", "gpsimd"):
        b.wait_only(_e, [cdep])

    b.phase = "P1"

    t_tile = at("p1_t", [128, 1024], F32, SCR0)
    stats = at("p1_stats", [128, 8], F32, SCR0 + 4096)
    sqscr = at("p1_sq", [128, 1024], F32, SCR0 + 4128)

    x_deps = []
    for t in range(NTT):
        x_deps.append(b.emit("sync", lambda E, t=t:
                             E.dma_start(out=XQ[:, 1024 * t:1024 * (t + 1)],
                                         in_=din["x"].ap()[128 * t:128 * (t + 1), :]),
                             inc="dX", dma=True))
    x_all = x_deps[-1]

    XNhi3 = XN_hi[:].rearrange("p (j c) -> p j c", c=2048)
    XNlo3 = XN_lo[:].rearrange("p (j c) -> p j c", c=2048)
    xn_done = []
    prev_ev = {0: None, 1: None}
    for t in range(NTT):
        xs = XQ[:, 1024 * t:1024 * (t + 1)]
        dx = [x_all]
        d1 = b.emit("vector", lambda E, xs=xs: E.reduce_sum(stats[:, 0:1], xs, AX.X),
                    deps=dx, inc="dv")
        d2 = b.emit("scalar", lambda E, xs=xs: E.activation(sqscr[:], xs, AF.Square,
                                                            accum_out=stats[:, 1:2]),
                    deps=dx, inc="sc")
        d3 = b.emit("vector", lambda E: E.tensor_scalar(stats[:, 2:3], stats[:, 0:1],
                                                        1.0 / D, None, ALU.mult),
                    deps=[d1], inc="dv")
        d4 = b.emit("vector", lambda E: E.tensor_tensor(stats[:, 3:4], stats[:, 2:3],
                                                        stats[:, 2:3], ALU.mult),
                    deps=[d3], inc="dv")
        d5 = b.emit("vector", lambda E: E.scalar_tensor_tensor(stats[:, 4:5], stats[:, 1:2],
                                                               1.0 / D, stats[:, 3:4],
                                                               ALU.mult, ALU.subtract),
                    deps=[d2, d4], inc="dv")
        d6 = b.emit("scalar", lambda E: E.activation(stats[:, 5:6], stats[:, 4:5],
                                                     AF.Sqrt, bias=EPS_LN),
                    deps=[d5], inc="sc")
        b.emit("vector", lambda E: E.drain(), deps=[d6])
        d7 = b.emit("vector", lambda E: E.reciprocal(stats[:, 6:7], stats[:, 5:6]),
                    deps=[d6], inc="dv")
        d8 = b.emit("vector", lambda E, xs=xs: E.tensor_scalar(t_tile[:], xs, stats[:, 2:3],
                                                               stats[:, 6:7], ALU.subtract,
                                                               ALU.mult),
                    deps=[d7], inc="dv")
        for half in range(2):
            bi = half
            deps_tr = [d8, cdep]
            if prev_ev[half] is not None:
                deps_tr.append(prev_ev[half])
            if bk.rdep[bi] is not None:
                deps_tr.append(bk.rdep[bi])
            dtr = None
            for jj in range(4):
                j = half * 4 + jj
                dtr = b.emit("tensor", lambda E, j=j, jj=jj, bi=bi:
                             E.transpose(bk.sl(bi, 128, 128, 128 * jj),
                                         t_tile[:, 128 * j:128 * (j + 1)], iden[:]),
                             deps=deps_tr if jj == 0 else (),
                             inc="pe" if jj == 3 else None)
            bk.wdep[bi] = dtr
            src3 = bk.sl(bi, 128, 512).rearrange("p (j c) -> p j c", c=128)
            dhi = read_bank(bi, "scalar", lambda E, half=half, t=t, src3=src3:
                            E.activation(XNhi3[:, 4 * half:4 * half + 4, 128 * t:128 * (t + 1)],
                                         src3, AF.Copy))
            dlo = read_bank(bi, "vector", lambda E, half=half, t=t, src3=src3:
                            E.tensor_tensor(XNlo3[:, 4 * half:4 * half + 4, 128 * t:128 * (t + 1)],
                                            src3,
                                            XNhi3[:, 4 * half:4 * half + 4, 128 * t:128 * (t + 1)],
                                            ALU.subtract), deps=[dhi])
            prev_ev[half] = dlo
            xn_done.append(dlo)

    xn_stash = []
    for q in range(NQ):
        for (src3, dst) in ((XNhi3, xnscr_hi), (XNlo3, xnscr_lo)):
            xn_stash.append(b.emit("scalar", lambda E, q=q, src3=src3, dst=dst:
                                   E.dma_start(out=dst.ap()[q],
                                               in_=src3[:, :, QT * q:QT * (q + 1)]),
                                   deps=xn_done, inc="dStXn", dma=True))
    xn_tap_deps = []
    if "tap_xnhi" in tap_t:
        xn_tap_deps.append(b.emit("sync", lambda E:
                                  E.dma_start(out=tap_t["tap_xnhi"].ap()[:], in_=XN_hi[:]),
                                  deps=xn_done, inc="dT_xnhi", dma=True))
    if "tap_xnlo" in tap_t:
        xn_tap_deps.append(b.emit("sync", lambda E:
                                  E.dma_start(out=tap_t["tap_xnlo"].ap()[:], in_=XN_lo[:]),
                                  deps=xn_done, inc="dT_xnlo", dma=True))
    if upto < 2:
        return _ret()

    b.phase = "P1b"

    WA = at("gt_wa", [16, T], F32, SCR0)
    WB = at("gt_wb", [16, T], F32, SCR0 + 8192)
    WC = at("gt_wc", [16, T], F32, SCR0 + 16384)
    wgt_hi = at("wgt_hi", [128, 128], BF16, SCR0 + 24576)
    wgt_lo = at("wgt_lo", [128, 128], BF16, SCR0 + 24832)
    wgt_st = at("wgt_st", [128, 16], F32, SCR0 + 25088)
    fsl = [at(f"gt_fsl{i}", [16, 512], F32, SCR0 + 25600 + 2048 * i) for i in range(2)]

    wg_deps = []
    prev_lo = None
    for k in range(8):
        dh_ = b.emit("gpsimd", lambda E, k=k:
                     E.dma_start(out=wgt_hi[:, 16 * k:16 * (k + 1)],
                                 in_=din["wgt"].ap()[128 * k:128 * (k + 1), :]),
                     inc="dWgt", dma=True)
        ds_ = b.emit("gpsimd", lambda E, k=k:
                     E.dma_start(out=wgt_st[:], in_=din["wgt"].ap()[128 * k:128 * (k + 1), :]),
                     deps=[prev_lo] if prev_lo else (), inc="dWgt", dma=True)
        prev_lo = b.emit("gpsimd", lambda E, k=k:
                         E.tensor_tensor(wgt_lo[:, 16 * k:16 * (k + 1)], wgt_st[:],
                                         wgt_hi[:, 16 * k:16 * (k + 1)], ALU.subtract),
                         deps=[dh_, ds_], inc="gp")
        wg_deps.append(prev_lo)

    sig_deps = []
    for n in range(4):
        bi = 2 + (n % 2)
        mms = []
        for k in range(8):
            for wt, xt in ((wgt_hi, XN_hi), (wgt_hi, XN_lo), (wgt_lo, XN_hi)):
                mms.append(lambda E, o, s0, s1, n=n, k=k, wt=wt, xt=xt:
                           E.matmul(o, wt[:, 16 * k:16 * (k + 1)],
                                    xt[:, 2048 * k + 512 * n:2048 * k + 512 * (n + 1)],
                                    start=s0, stop=s1))
        mm_group(bi, mms, deps=wg_deps + xn_done, prows=16)
        sig_deps.append(read_bank(bi, "scalar", lambda E, n=n, bi=bi:
                                  E.activation(WC[:, 512 * n:512 * (n + 1)],
                                               bk.sl(bi, 16), AF.Sigmoid),
                                  deps=xn_done))
    dln = b.emit("scalar", lambda E: E.activation(WB[:], WC[:], AF.Ln),
                 deps=sig_deps + xn_done, inc="sc")

    cur, nxt = WB, WA
    dcur_l = [dln]
    for s in range(8):
        step = 1 << s
        ds_list = []
        for c in range(NCH):
            b0 = CH * c
            ds_list.append(b.emit("vector", lambda E, cur=cur, nxt=nxt, b0=b0, step=step:
                                  E.tensor_tensor(nxt[:, b0 + step:b0 + CH],
                                                  cur[:, b0 + step:b0 + CH],
                                                  cur[:, b0:b0 + CH - step], ALU.add),
                                  deps=dcur_l, inc="dv"))
            ds_list.append(b.emit("scalar", lambda E, cur=cur, nxt=nxt, b0=b0, step=step:
                                  E.activation(nxt[:, b0:b0 + step], cur[:, b0:b0 + step],
                                               AF.Copy),
                                  deps=dcur_l, inc="sc"))
        dcur_l = ds_list
        cur, nxt = nxt, cur
    C = cur
    assert C is WB
    dC = b.emit("vector", lambda E: E.tensor_copy(Drow[:], C[:, T - 1:T]),
                deps=dcur_l, inc="dv")
    Cends = C[:].rearrange("p (c z) -> p c z", z=CH)[:, :, CH - 1:CH].rearrange("p c z -> p (c z)")
    dEB = b.emit("scalar", lambda E: E.activation(expB[:], Cends, AF.Exp, scale=1.0 / GN),
                 deps=[dC], inc="sc")
    dBs = b.emit("vector", lambda E: E.reduce_sum(Drow[:], Cends, AX.X), deps=[dC], inc="dv")
    dD = b.emit("scalar", lambda E: E.activation(Drow2[:], Drow[:], AF.Exp, scale=1.0 / GN),
                deps=[dBs], inc="sc")
    dbx = []
    for c in range(NCH):
        dbx.append(b.emit("vector", lambda E, c=c:
                          E.tensor_scalar(WC[:, CH * c:CH * (c + 1)], C[:, CH * c:CH * (c + 1)],
                                          0.0, C[:, CH * c + CH - 1:CH * c + CH],
                                          ALU.mult, ALU.add),
                          deps=[dC], inc="dv"))
    dU = b.emit("vector", lambda E: E.tensor_tensor(WA[:], WC[:], C[:], ALU.subtract),
                deps=dbx, inc="dv")
    Ubuf = WA
    debf0 = b.emit("sync", lambda E: E.dma_start(
        out=ebscr.ap().rearrange("z (p c) -> (z p) c", p=16), in_=expB[:]),
        deps=[dEB], inc="dEbf", dma=True)
    if "tap_gt" in tap_t:
        b.emit("sync", lambda E: E.dma_start(out=tap_t["tap_gt"].ap()[:, 0:T], in_=WA[:]),
               deps=[dU], inc="dT_gt", dma=True)
        b.emit("sync", lambda E: E.dma_start(out=tap_t["tap_gt"].ap()[:, T:2 * T], in_=WB[:]),
               deps=[dU], inc="dT_gt", dma=True)
    if upto < 3:
        return _ret()

    b.phase = "P2"

    kch = [at(f"p2_kc{i}", [128, 2048], F32, SCR0 + 8192 * (i + 1)) for i in range(2)]
    sq2 = at("p2_sq", [128, 512], F32, SCR0 + 40960)
    wst2 = [at(f"p2_wst{i}", [128, 512], F32, SCR0 + 45056 + 2048 * i) for i in range(2)]
    statrow = [r_stA, r_stB]
    stat_state = {"i": 0, "dep": None}

    def stat_reset():
        stat_state["i"] = 0
        stat_state["dep"] = b.emit("vector", lambda E: E.memset(statrow[0][:, :], 0.0),
                                   deps=[stat_state["dep"]], inc="dv")

    wst_war = [None, None]

    def load_whilo(wname, whi, wlo, deps=()):
        semh = "dwh_" + wname
        seml = "dws_" + wname
        dhk = []
        for k in range(8):
            dhk.append(b.emit("gpsimd", lambda E, k=k, whi=whi:
                              E.dma_start(out=whi[:, 1024 * k:1024 * (k + 1)],
                                          in_=din[wname].ap()[128 * k:128 * (k + 1), :]),
                              deps=deps, inc=semh, dma=True))
        lastv = None
        for j in range(16):
            k, h2 = j // 2, j % 2
            st = wst2[j % 2]
            ds_ = b.emit("sync", lambda E, k=k, h2=h2, st=st:
                         E.dma_start(out=st[:],
                                     in_=din[wname].ap()[128 * k:128 * (k + 1),
                                                         512 * h2:512 * (h2 + 1)]),
                         deps=list(deps) + [wst_war[j % 2]], inc=seml + str(j % 2),
                         dma=True)
            lastv = b.emit("vector", lambda E, k=k, h2=h2, st=st, whi=whi, wlo=wlo:
                           E.tensor_tensor(wlo[:, 1024 * k + 512 * h2:1024 * k + 512 * (h2 + 1)],
                                           st[:],
                                           whi[:, 1024 * k + 512 * h2:1024 * k + 512 * (h2 + 1)],
                                           ALU.subtract),
                           deps=[ds_, dhk[-1]] + list(deps), inc="dv")
            wst_war[j % 2] = lastv
        return [lastv, dhk[-1]]

    def load_whi_only(wname, whi, deps=()):
        sem = "dwh_" + wname
        dh_ = None
        for k in range(8):
            dh_ = b.emit("gpsimd", lambda E, k=k, whi=whi:
                         E.dma_start(out=whi[:, 1024 * k:1024 * (k + 1)],
                                     in_=din[wname].ap()[128 * k:128 * (k + 1), :]),
                         deps=deps, inc=sem, dma=True)
        return dh_

    def load_wlo_only(wname, whi, wlo, stpair, war, deps=()):
        sem = "dwl_" + wname
        lastv = None
        for j in range(16):
            k, h2 = j // 2, j % 2
            st = stpair[j % 2]
            ds_ = b.emit("sync", lambda E, k=k, h2=h2, st=st:
                         E.dma_start(out=st[:],
                                     in_=din[wname].ap()[128 * k:128 * (k + 1),
                                                         512 * h2:512 * (h2 + 1)]),
                         deps=list(deps) + [war[j % 2]], inc=sem + str(j % 2), dma=True)
            lastv = b.emit("vector", lambda E, k=k, h2=h2, st=st, whi=whi, wlo=wlo:
                           E.tensor_tensor(wlo[:, 1024 * k + 512 * h2:1024 * k + 512 * (h2 + 1)],
                                           st[:],
                                           whi[:, 1024 * k + 512 * h2:1024 * k + 512 * (h2 + 1)],
                                           ALU.subtract),
                           deps=[ds_] + list(deps), inc="dv")
            war[j % 2] = lastv
        return lastv

    dWk = load_whilo("wk", WSa_hi, WSa_lo, deps=xn_done)
    dWv = load_whilo("wv", WSb_hi, WSb_lo, deps=xn_done)

    def proj_fm(whi, wlo, wdep, stash_dram, semp, war0=(None, None)):
        stash_deps = []
        stash_war = list(war0)
        for m in range(8):
            buf = kch[m % 2]
            base_bi = 0 if m % 2 == 0 else 4
            ev_deps = []
            sq_deps = []
            for n in range(4):
                bi = base_bi + n
                mms = []
                for k in range(8):
                    for wt, xt in ((whi, XN_hi), (whi, XN_lo), (wlo, XN_hi)):
                        mms.append(lambda E, o, s0, s1, k=k, m=m, n=n, wt=wt, xt=xt:
                                   E.matmul(o, wt[:, 1024 * k + 128 * m:1024 * k + 128 * (m + 1)],
                                            xt[:, 2048 * k + 512 * n:2048 * k + 512 * (n + 1)],
                                            start=s0, stop=s1))
                mm_group(bi, mms, deps=list(wdep) + xn_done)
                dev = read_bank(bi, "scalar", lambda E, n=n, bi=bi, buf=buf:
                                E.activation(buf[:, 512 * n:512 * (n + 1)], bk.sl(bi), AF.Copy),
                                deps=[dC, dU] + b.tapdeps
                                + ([stash_war[m % 2]] if stash_war[m % 2] else []))
                ev_deps.append(dev)
                dsq = b.emit("scalar", lambda E, n=n, buf=buf:
                             E.activation(sq2[:], buf[:, 512 * n:512 * (n + 1)], AF.Square),
                             deps=[dev] + sq_deps[-1:], inc="sc")
                mm_group(bi, [lambda E, o, s0, s1:
                              E.matmul(o, onesall[:, 0:1], sq2[:], start=s0, stop=s1)],
                         deps=[dsq], prows=1)
                i = stat_state["i"]
                src_r, dst_r = statrow[i % 2], statrow[1 - i % 2]
                sdep = read_bank(bi, "vector",
                                 lambda E, n=n, bi=bi, src_r=src_r, dst_r=dst_r:
                                 E.tensor_tensor(dst_r[:, 512 * n:512 * (n + 1)],
                                                 src_r[:, 512 * n:512 * (n + 1)],
                                                 bk.sl(bi, 1), ALU.add),
                                 deps=[stat_state["dep"]])
                sq_deps = [sdep]
                stat_state["dep"] = sdep
            stat_state["i"] += 1
            stash_deps.append(b.emit("scalar", lambda E, m=m, buf=buf, stash_dram=stash_dram:
                                     E.dma_start(out=stash_dram.ap()[128 * m:128 * (m + 1), :],
                                                 in_=buf[:]),
                                     deps=ev_deps, inc=semp + str(m % 2), dma=True))
            stash_war[m % 2] = stash_deps[-1]
        return stash_deps

    stat_reset()
    k_stash = proj_fm(WSa_hi, WSa_lo, dWk, kscr, "dStK")
    kfinal = statrow[1 - (stat_state["i"] - 1) % 2]
    d_kf = stat_state["dep"]
    d_sk = b.emit("scalar", lambda E: E.activation(r_stC[:, :], kfinal[:, :], AF.Sqrt,
                                                   bias=EPS_RMS, scale=1.0 / D),
                  deps=[d_kf], inc="sc")
    stat_state["dep"] = d_sk
    stat_reset()
    v_stash = proj_fm(WSb_hi, WSb_lo, dWv, vscr, "dStV", war0=k_stash[-2:])
    vfinal = statrow[1 - (stat_state["i"] - 1) % 2]
    vother = statrow[(stat_state["i"] - 1) % 2]
    d_vf = stat_state["dep"]
    d_sv = b.emit("scalar", lambda E: E.activation(vother[:, :], vfinal[:, :], AF.Sqrt,
                                                   bias=EPS_RMS, scale=1.0 / D),
                  deps=[d_vf, d_sk], inc="sc")
    d_m1 = b.emit("vector", lambda E: E.tensor_tensor(vfinal[:, :], r_stC[:, :],
                                                      vother[:, :], ALU.mult),
                  deps=[d_sv], inc="dv")
    b.emit("vector", lambda E: E.drain(), deps=[d_m1])
    d_rkv = b.emit("vector", lambda E: E.reciprocal(r_stC[:, :], vfinal[:, :]),
                   deps=[d_m1], inc="dv")
    fk_st = []
    for n in range(4):
        bi = 2 + (n % 2)
        mm_group(bi, [lambda E, o, s0, s1, n=n:
                      E.matmul(o, onesall[0:1, 0:16], r_stC[:, 512 * n:512 * (n + 1)],
                               start=s0, stop=s1)],
                 deps=[d_rkv], prows=16)
        de1 = b.emit("scalar", lambda E, n=n:
                     E.activation(fsl[0][:], Ubuf[:, 512 * n:512 * (n + 1)], AF.Exp,
                                  scale=1.0 / GN),
                     deps=[dU] + fk_st[-1:], inc="sc")
        dfk = read_bank(bi, "vector", lambda E, n=n, bi=bi:
                        E.tensor_tensor(fsl[1][:], fsl[0][:], bk.sl(bi, 16), ALU.mult),
                        deps=[de1] + fk_st[-1:])
        fk_st.append(b.emit("scalar", lambda E, n=n:
                            E.dma_start(out=fkscr.ap()[:, 512 * n:512 * (n + 1)],
                                        in_=fsl[1][:]),
                            deps=[dfk], inc="dFkSt", dma=True))
    d_fkd = fk_st[-1]
    if "tap_fk" in tap_t:
        b.emit("sync", lambda E: E.dma_start(out=tap_t["tap_fk"].ap()[:], in_=fkscr.ap()[:]),
               deps=[d_fkd], inc="dT_fk", dma=True)
    if "tap_k" in tap_t:
        b.emit("sync", lambda E: E.dma_start(out=tap_t["tap_k"].ap()[:], in_=kscr.ap()[:]),
               deps=k_stash[-2:], inc="dT_k", dma=True)
    if "tap_v" in tap_t:
        b.emit("sync", lambda E: E.dma_start(out=tap_t["tap_v"].ap()[:], in_=vscr.ap()[:]),
               deps=v_stash[-2:], inc="dT_v", dma=True)
    if upto < 4:
        return _ret()

    b.phase = "P4a"

    WQ_hi = at("WQ_hi", [128, 8192], BF16, SCR0 + 8192)
    WQ_lo = at("WQ_lo", [128, 8192], BF16, SCR0 + 24576)
    dWq = load_whilo("wq", WQ_hi, WQ_lo,
                     deps=v_stash[-2:] + k_stash[-2:] + [dC, dU, d_fkd])
    dWq_l = list(dWq)
    wq_stage_dead = [wst_war[0], wst_war[1]]
    stat_state["dep"] = d_rkv
    stat_reset()
    q_sq_deps = []
    q_done = []
    for m in range(8):
        base_bi = 0 if m % 2 == 0 else 4
        for n in range(4):
            bi = base_bi + n
            mms = []
            for k in range(8):
                for wt, xt in ((WQ_hi, XN_hi), (WQ_hi, XN_lo), (WQ_lo, XN_hi)):
                    mms.append(lambda E, o, s0, s1, k=k, m=m, n=n, wt=wt, xt=xt:
                               E.matmul(o, wt[:, 1024 * k + 128 * m:1024 * k + 128 * (m + 1)],
                                        xt[:, 2048 * k + 512 * n:2048 * k + 512 * (n + 1)],
                                        start=s0, stop=s1))
            mm_group(bi, mms, deps=dWq_l)
            dev = read_bank(bi, "scalar", lambda E, m=m, n=n, bi=bi:
                            E.activation(XQ[:, 2048 * m + 512 * n:2048 * m + 512 * (n + 1)],
                                         bk.sl(bi), AF.Copy))
            q_done.append(dev)
            dsq = b.emit("scalar", lambda E, m=m, n=n:
                         E.activation(sq2[:], XQ[:, 2048 * m + 512 * n:2048 * m + 512 * (n + 1)],
                                      AF.Square), deps=[dev] + q_sq_deps[-1:], inc="sc")
            mm_group(bi, [lambda E, o, s0, s1:
                          E.matmul(o, onesall[:, 0:1], sq2[:], start=s0, stop=s1)],
                     deps=[dsq], prows=1)
            i = stat_state["i"]
            src_r, dst_r = statrow[i % 2], statrow[1 - i % 2]
            sdep = read_bank(bi, "vector", lambda E, n=n, bi=bi, src_r=src_r, dst_r=dst_r:
                             E.tensor_tensor(dst_r[:, 512 * n:512 * (n + 1)],
                                             src_r[:, 512 * n:512 * (n + 1)],
                                             bk.sl(bi, 1), ALU.add),
                             deps=[stat_state["dep"]])
            q_sq_deps = [sdep]
            stat_state["dep"] = sdep
        stat_state["i"] += 1
    qfinal = statrow[1 - (stat_state["i"] - 1) % 2]
    d_qf = stat_state["dep"]
    d_mp = b.emit("vector", lambda E: E.tensor_scalar(r_stC[:, :], qfinal[:, :], 1.0 / D,
                                                      EPS_RMS, ALU.mult, ALU.add),
                  deps=[d_qf, d_fkd], inc="dv")
    eps_st = []
    for n in range(4):
        bi = 2 + (n % 2)
        mm_group(bi, [lambda E, o, s0, s1, n=n:
                      E.matmul(o, onesall[0:1, 0:16], r_stC[:, 512 * n:512 * (n + 1)],
                               start=s0, stop=s1)],
                 deps=[d_mp], prows=16)
        de2 = b.emit("scalar", lambda E, n=n:
                     E.activation(fsl[0][:], Ubuf[:, 512 * n:512 * (n + 1)], AF.Exp,
                                  scale=2.0 / GN),
                     deps=[dU, d_fkd] + eps_st[-1:], inc="sc")
        dep_ = read_bank(bi, "vector", lambda E, n=n, bi=bi:
                         E.scalar_tensor_tensor(fsl[1][:], fsl[0][:], float(DH) * EPS_RMS,
                                                bk.sl(bi, 16), ALU.mult, ALU.mult),
                         deps=[de2] + eps_st[-1:])
        eps_st.append(b.emit("scalar", lambda E, n=n:
                             E.dma_start(out=epsscr.ap()[:, 512 * n:512 * (n + 1)],
                                         in_=fsl[1][:]),
                             deps=[dep_], inc="dEpSt", dma=True))
    d_epsd = eps_st[-1]
    tap_q_dep = None
    if "tap_eps" in tap_t:
        b.emit("sync", lambda E: E.dma_start(out=tap_t["tap_eps"].ap()[:], in_=epsscr.ap()[:]),
               deps=[d_epsd], inc="dT_eps", dma=True)
    if "tap_q" in tap_t:
        tap_q_dep = b.emit("sync", lambda E: E.dma_start(out=tap_t["tap_q"].ap()[:], in_=XQ[:]),
                           deps=q_done, inc="dT_q", dma=True)
    dWgh = load_whi_only("wg", WGg_hi, deps=q_done + xn_stash[-1:] + xn_tap_deps)
    dWoh = load_whi_only("wo", WGo_hi, deps=q_done + xn_stash[-1:] + xn_tap_deps)
    if upto < 5:
        return _ret()

    b.phase = "P3"

    fb = [at(f"p3_fb{i}", [128, 2048], F32, SCR0 + 8192 * i) for i in range(2)]
    kd_fm = at("p3_kdfm", [128, 2048], F32, SCR0 + 16384)
    kd_tm = at("p3_kdtm", [128, 1024], F32, SCR0 + 24576)
    v_tm = at("p3_vtm", [128, 1024], F32, SCR0 + 28672)
    KV_all = at("p3_kv", [64, 1024], F32, SCR0 + 32768)
    Sbuf = [at(f"p3_s{i}", [128, 64], F32, SCR0 + 36864 + 256 * i) for i in range(2)]
    eBr_all = at("p3_ebr", [128, 128], F32, SCR0 + 37376)
    Ss_all = at("p3_ssall", [128, 512], F32, SCR0 + 37888)
    o_head = at("p5_oh", [128, 2048], F32, SCR0 + 39936)
    Dflat = at("p3_dflat", [1, 1024], F32, SCR0 + 39936)
    KVg = at("p4_kvg", [64, 1024], F32, SCR0 + 39936)
    B1 = at("p4_b1", [64, 1024], F32, SCR0 + 44032)

    debf = b.emit("sync", lambda E: E.dma_start(out=r32A[:, 0:128], in_=ebscr.ap()),
                  deps=[debf0, d_epsd], inc="dEbf", dma=True)
    mm_group(6, [lambda E, o, s0, s1:
                 E.matmul(o, onesall[32:33, 0:64], r32A[:, 0:128], start=s0, stop=s1,
                          **tp(32, 0))],
             deps=[debf], prows=64, cols=128)
    read_bank(6, "vector", lambda E: E.tensor_copy(eBr_all[0:64, :], bk.sl(6, 64, 128)),
              deps=[d_epsd])
    mm_group(7, [lambda E, o, s0, s1:
                 E.matmul(o, onesall[32:33, 0:64], r32A[:, 0:128], start=s0, stop=s1,
                          **tp(32, 64))],
             deps=[debf], prows=64, cols=128, poff=64)
    dEBr = read_bank(7, "vector", lambda E:
                     E.tensor_copy(eBr_all[64:128, :], bk.sl(7, 64, 128, poff=64)),
                     deps=[d_epsd])
    if p3sub == 1:
        return _ret()

    hp_war = [None, None]
    stg_war = [None]
    kd_war = [None]
    vtm_war = [None]

    def head_prep(h, dep_extra=()):
        qrow = 64 * (h % 2)
        vrow = 64 - qrow
        buf = fb[h % 2]
        war = list(hp_war[h % 2]) if hp_war[h % 2] else []
        dk = b.emit("sync", lambda E, h=h, buf=buf, qrow=qrow:
                    E.dma_start(out=buf[qrow:qrow + 64, :],
                                in_=kscr.ap()[64 * h:64 * (h + 1), :]),
                    deps=k_stash[-2:] + v_stash[-2:] + war + [d_epsd] + b.tapdeps
                    + list(dep_extra), inc="dFk%d" % (h % 2), dma=True)
        dv_ = b.emit("sync", lambda E, h=h, buf=buf, vrow=vrow:
                     E.dma_start(out=buf[vrow:vrow + 64, :],
                                 in_=vscr.ap()[64 * h:64 * (h + 1), :]),
                     deps=k_stash[-2:] + v_stash[-2:] + war + [d_epsd] + list(dep_extra),
                     inc="dFv%d" % (h % 2), dma=True)
        dstg = b.emit("sync", lambda E, h=h:
                      E.dma_start(out=r_stA[:, :], in_=fkscr.ap()[h:h + 1, :]),
                      deps=[d_fkd, d_epsd] + ([stg_war[0]] if stg_war[0] else []),
                      inc="dFkR", dma=True)
        dkds = []
        for n in range(4):
            bi = n % 2
            mm_group(bi, [lambda E, o, s0, s1, n=n, qrow=qrow:
                          E.matmul(o, onesall[0:1, 0:64], r_stA[:, 512 * n:512 * (n + 1)],
                                   start=s0, stop=s1, **tp(0, qrow))],
                     deps=[dstg], prows=64, poff=qrow)
            dkds.append(read_bank(bi, "vector",
                                  lambda E, n=n, bi=bi, buf=buf, qrow=qrow:
                                  E.tensor_tensor(kd_fm[qrow:qrow + 64, 512 * n:512 * (n + 1)],
                                                  buf[qrow:qrow + 64, 512 * n:512 * (n + 1)],
                                                  bk.sl(bi, 64, 512, poff=qrow), ALU.mult),
                                  deps=[dk] + ([kd_war[0]] if kd_war[0] else [])))
        stg_war[0] = dkds[-1]
        dkdtm = []
        for half in range(2):
            bi = half
            deps_tr = dkds + ([bk.rdep[bi]] if bk.rdep[bi] else [])
            dtr = None
            for jj in range(8):
                tt8 = half * 8 + jj
                dtr = b.emit("tensor", lambda E, tt8=tt8, jj=jj, bi=bi, qrow=qrow:
                             E.transpose(bk.sl(bi, 128, 64, 64 * jj),
                                         kd_fm[qrow:qrow + 64, 128 * tt8:128 * (tt8 + 1)],
                                         iden[qrow:qrow + 64, qrow:qrow + 64]),
                             deps=deps_tr if jj == 0 else (), inc="pe" if jj == 7 else None)
            bk.wdep[bi] = dtr
            dkdtm.append(read_bank(bi, "scalar", lambda E, half=half, bi=bi:
                                   E.activation(kd_tm[:, 512 * half:512 * (half + 1)],
                                                bk.sl(bi), AF.Copy)))
        dvtm = []
        for half in range(2):
            bi = 2 + half
            deps_tr = [dv_] + ([bk.rdep[bi]] if bk.rdep[bi] else []) + \
                ([vtm_war[0]] if vtm_war[0] else [])
            dtr = None
            for jj in range(8):
                tt8 = half * 8 + jj
                dtr = b.emit("tensor", lambda E, tt8=tt8, jj=jj, bi=bi, buf=buf, vrow=vrow:
                             E.transpose(bk.sl(bi, 128, 64, 64 * jj),
                                         buf[vrow:vrow + 64, 128 * tt8:128 * (tt8 + 1)],
                                         iden[vrow:vrow + 64, vrow:vrow + 64]),
                             deps=deps_tr if jj == 0 else (), inc="pe" if jj == 7 else None)
            bk.wdep[bi] = dtr
            dvtm.append(read_bank(bi, "scalar", lambda E, half=half, bi=bi:
                                  E.activation(v_tm[:, 512 * half:512 * (half + 1)],
                                               bk.sl(bi), AF.Copy)))
        return {"buf": buf, "kdtm": dkdtm, "vtm": dvtm, "k": dk, "v": dv_,
                "kdfm": dkds, "qrow": qrow}

    dKV = []
    sw1_war = [None, None]
    for h in range(H):
        hp = head_prep(h)
        qrow = hp["qrow"]
        dS = b.emit("vector", lambda E, qrow=qrow: E.memset(Sbuf[0][qrow:qrow + 64, :], 0.0),
                    deps=[sw1_war[h % 2]] if sw1_war[h % 2] else [d_epsd], inc="dv")
        sbk = 6 + (h % 2)
        alld = hp["kdtm"] + hp["vtm"] + ([bk.rdep[sbk]] if bk.rdep[sbk] else [])
        pe_dep = None
        for c in range(NCH):
            for j in range(2):
                pe_dep = b.emit("tensor", lambda E, c=c, j=j, qrow=qrow, sbk=sbk:
                                E.matmul(bk.sl(sbk, 64, 64, 64 * c, qrow),
                                         kd_tm[:, 64 * (2 * c + j):64 * (2 * c + j + 1)],
                                         v_tm[:, 64 * (2 * c + j):64 * (2 * c + j + 1)],
                                         start=(j == 0), stop=(j == 1), **tp(0, qrow)),
                                deps=alld if (c == 0 and j == 0) else (),
                                inc="pe" if (c == NCH - 1 and j == 1) else None)
        bk.wdep[sbk] = pe_dep
        dprev = dS
        for c in range(NCH):
            dts = b.emit("vector", lambda E, c=c, h=h, qrow=qrow:
                         E.tensor_scalar(Sbuf[1][qrow:qrow + 64, :],
                                         Sbuf[0][qrow:qrow + 64, :],
                                         eBr_all[qrow:qrow + 64, 8 * h + c:8 * h + c + 1],
                                         None, ALU.mult),
                         deps=[dprev, dEBr], inc="dv")
            dprev = b.emit("vector", lambda E, c=c, qrow=qrow, sbk=sbk:
                           E.tensor_tensor(Sbuf[0][qrow:qrow + 64, :],
                                           Sbuf[1][qrow:qrow + 64, :],
                                           bk.sl(sbk, 64, 64, 64 * c, qrow), ALU.add),
                           deps=[dts, pe_dep], inc="dv")
        bk.rdep[sbk] = dprev
        dcp = b.emit("scalar", lambda E, h=h, qrow=qrow:
                     E.activation(KV_all[:, 64 * h:64 * (h + 1)],
                                  Sbuf[0][qrow:qrow + 64, :], AF.Copy),
                     deps=[dprev], inc="sc")
        dKV.append(dcp)
        sw1_war[h % 2] = dcp
        hp_war[h % 2] = [hp["vtm"][-1], hp["kdfm"][-1]]
        kd_war[0] = hp["kdtm"][-1]
        vtm_war[0] = hp["kdtm"][-1]
        if p3sub == 2 and h == 0:
            return _ret()
    if p3sub == 3:
        return _ret()

    dz_df = b.emit("vector", lambda E: E.memset(Dflat[:], 0.0), deps=[d_epsd], inc="dv")
    mm_group(6, [lambda E, o, s0, s1: E.transpose(o, Drow2[:], iden[0:16, 0:16])],
             deps=[dD], prows=1, cols=16)
    dDf = read_bank(6, "vector", lambda E: E.tensor_copy(Dflat[:, 0:16], bk.sl(6, 1, 16)),
                    deps=[dz_df])
    dp1 = b.emit("scalar", lambda E: E.dma_start(out=cc_in.ap()[0:64, :], in_=KV_all[:]),
                 deps=dKV, inc="dCCp", dma=True)
    dp2 = b.emit("scalar", lambda E: E.dma_start(out=cc_in.ap()[64:65, :], in_=Dflat[:]),
                 deps=[dDf], inc="dCCp", dma=True)
    dcc = b.emit("gpsimd", lambda E:
                 E.collective_compute("AllGather", ALU.bypass,
                                      replica_groups=[[0, 1, 2, 3], [4, 5, 6, 7]],
                                      ins=[cc_in.ap().opt()], outs=[cc_out.ap().opt()]),
                 deps=[dp1, dp2], inc="cc")
    b.wait_only("gpsimd", [dcc])
    if "tap_kv" in tap_t:
        b.emit("sync", lambda E: E.dma_start(out=tap_t["tap_kv"].ap()[:], in_=KV_all[:]),
               deps=dKV, inc="dT_kv", dma=True)
    if upto < 6:
        return _ret()

    b.phase = "P4b"
    if p3sub == 4:
        b.wait_only("sync", [dWgh, dWoh])
        return _ret()

    dDs = b.emit("sync", lambda E: E.dma_start(
        out=r32A[:, 0:64],
        in_=cc_out.ap().rearrange("(r z) c -> r z c", z=65)[:, 64, 0:16]),
        deps=[dcc, dEBr], inc="dP4d", dma=True)
    mm_group(6, [lambda E, o, s0, s1:
                 E.matmul(o, onesall[32:33, 0:64], r32A[:, 0:64], start=s0, stop=s1,
                          **tp(32, 0))],
             deps=[dDs], prows=64, cols=64)
    dDr = read_bank(6, "vector", lambda E: E.tensor_copy(Drep[:], bk.sl(6, 64, 64)),
                    deps=[xn_stash[-1]])
    if p3sub == 5:
        return _ret()
    dk0 = b.emit("sync", lambda E: E.dma_start(out=B1[:], in_=cc_out.ap()[0:64, :]),
                 deps=[dcc] + wq_stage_dead, inc="dP4k", dma=True)
    dk1 = b.emit("sync", lambda E: E.dma_start(out=KVg[:], in_=cc_out.ap()[65:65 + 64, :]),
                 deps=[dcc], inc="dP4k", dma=True)
    dt_s0 = b.emit("vector", lambda E: E.tensor_scalar(B3[:], B1[:], selrep[:, 0:1], None,
                                                       ALU.mult),
                   deps=[dk0, dk1, cdep, xn_stash[-1]], inc="dv")
    dp2s = []
    for h in range(H):
        dp2s.append(b.emit("vector", lambda E, h=h:
                           E.scalar_tensor_tensor(B2[:, 64 * h:64 * (h + 1)],
                                                  B1[:, 64 * h:64 * (h + 1)],
                                                  Drep[:, 16 + h:16 + h + 1],
                                                  KVg[:, 64 * h:64 * (h + 1)],
                                                  ALU.mult, ALU.add),
                           deps=[dk0, dk1, dDr], inc="dv"))
    dt_s1 = b.emit("vector", lambda E: E.scalar_tensor_tensor(B1[:], B2[:], selrep[:, 1:2],
                                                              B3[:], ALU.mult, ALU.add),
                   deps=dp2s + [dt_s0], inc="dv")
    if p3sub == 6:
        return _ret()
    dk2 = b.emit("sync", lambda E: E.dma_start(out=KVg[:], in_=cc_out.ap()[130:130 + 64, :]),
                 deps=[dcc, dp2s[-1]], inc="dP4k", dma=True)
    dp3s = []
    for h in range(H):
        dp3s.append(b.emit("vector", lambda E, h=h:
                           E.scalar_tensor_tensor(B3[:, 64 * h:64 * (h + 1)],
                                                  B2[:, 64 * h:64 * (h + 1)],
                                                  Drep[:, 32 + h:32 + h + 1],
                                                  KVg[:, 64 * h:64 * (h + 1)],
                                                  ALU.mult, ALU.add),
                           deps=[dk2, dt_s1], inc="dv"))
    d_sinv = b.emit("vector", lambda E: E.scalar_tensor_tensor(B2[:], B3[:], selrep[:, 2:3],
                                                               B1[:], ALU.mult, ALU.add),
                    deps=dp3s, inc="dv")
    d_sin = b.emit("scalar", lambda E: E.dma_start(out=sinscr.ap()[:], in_=B2[:]),
                   deps=[d_sinv], inc="dSinSt", dma=True)
    if "tap_sin" in tap_t:
        b.emit("sync", lambda E: E.dma_start(out=tap_t["tap_sin"].ap()[:], in_=sinscr.ap()[:]),
               deps=[d_sin], inc="dT_sin", dma=True)
    if upto < 7:
        return _ret()

    b.phase = "P5"

    hp_war = [None, None]
    o_done = []
    sw2_war = [None, None]
    sa_war = [None]
    eps_stg_war = [None]
    dms = None
    for h in range(H):
        hp = head_prep(h, dep_extra=[d_sin])
        qrow = hp["qrow"]
        qcol = 2048 * (h // 2)
        dS0 = b.emit("sync", lambda E, h=h, qrow=qrow:
                     E.dma_start(out=Sbuf[0][qrow:qrow + 64, :],
                                 in_=sinscr.ap()[:, 64 * h:64 * (h + 1)]),
                     deps=[d_sin] + ([sw2_war[h % 2]] if sw2_war[h % 2] else []),
                     inc="dSin%d" % (h % 2), dma=True)
        sbk = 6 + (h % 2)
        alld = hp["kdtm"] + hp["vtm"] + ([bk.rdep[sbk]] if bk.rdep[sbk] else [])
        pe_dep = None
        for c in range(NCH):
            for j in range(2):
                pe_dep = b.emit("tensor", lambda E, c=c, j=j, qrow=qrow, sbk=sbk:
                                E.matmul(bk.sl(sbk, 64, 64, 64 * c, qrow),
                                         kd_tm[:, 64 * (2 * c + j):64 * (2 * c + j + 1)],
                                         v_tm[:, 64 * (2 * c + j):64 * (2 * c + j + 1)],
                                         start=(j == 0), stop=(j == 1), **tp(0, qrow)),
                                deps=alld if (c == 0 and j == 0) else (),
                                inc="pe" if (c == NCH - 1 and j == 1) else None)
        bk.wdep[sbk] = pe_dep
        cur = 0
        dprev = dS0
        dSs = []
        for c in range(NCH):
            dts = b.emit("vector", lambda E, c=c, cur=cur, h=h, qrow=qrow:
                         E.tensor_scalar(Ss_all[qrow:qrow + 64, 64 * c:64 * (c + 1)],
                                         Sbuf[cur][qrow:qrow + 64, :],
                                         eBr_all[qrow:qrow + 64, 8 * h + c:8 * h + c + 1],
                                         None, ALU.mult),
                         deps=[dprev, dEBr], inc="dv")
            dSs.append(dts)
            dprev = b.emit("vector", lambda E, c=c, cur=cur, qrow=qrow, sbk=sbk:
                           E.tensor_tensor(Sbuf[1 - cur][qrow:qrow + 64, :],
                                           Ss_all[qrow:qrow + 64, 64 * c:64 * (c + 1)],
                                           bk.sl(sbk, 64, 64, 64 * c, qrow), ALU.add),
                           deps=[dts, pe_dep], inc="dv")
            cur = 1 - cur
        bk.rdep[sbk] = dprev
        oh_deps = []
        SA_pair = (SA_sb, SA_sbB)
        for t2 in range(4):
            for ci in range(2):
                c = 2 * t2 + ci
                bi = 4 + ci
                alldeps = hp["kdfm"] + [q_done[-1], tap_q_dep] \
                    + ([bk.rdep[bi]] if bk.rdep[bi] else [])
                dep = None
                for jh in range(2):
                    dep = b.emit("tensor",
                                 lambda E, c=c, jh=jh, qrow=qrow, qcol=qcol, bi=bi:
                                 E.matmul(bk.sl(bi, 128, 256, 256 * jh),
                                          kd_fm[qrow:qrow + 64,
                                                256 * c + 128 * jh:256 * c + 128 * (jh + 1)],
                                          XQ[qrow:qrow + 64, qcol + 256 * c:qcol + 256 * (c + 1)],
                                          start=True, stop=True, **tp(qrow, 0)),
                                 deps=alldeps if jh == 0 else (),
                                 inc="pe" if jh == 1 else None)
                bk.wdep[bi] = dep
            dsa_all = []
            for ci in range(2):
                bi = 4 + ci
                sab = SA_pair[ci]
                dsa_all.append(read_bank(bi, "vector", lambda E, bi=bi, sab=sab:
                                         E.tensor_tensor(sab[:, :], bk.sl(bi, 128, 512),
                                                         mask[:, :], ALU.mult),
                                         deps=[cdep, xn_stash[-1]]
                                         + ([sa_war[0]] if sa_war[0] else [])))
            sa_war[0] = dsa_all[-1]
            for ci in range(2):
                c = 2 * t2 + ci
                sab = SA_pair[ci]
                bi = 4 + ci
                mms = [lambda E, o, s0, s1, c=c, qrow=qrow, sab=sab:
                       E.matmul(o, v_tm[:, 64 * 2 * c:64 * (2 * c + 1)],
                                sab[:, 0:256], start=s0, stop=s1, **tp(0, qrow)),
                       lambda E, o, s0, s1, c=c, qrow=qrow, sab=sab:
                       E.matmul(o, v_tm[:, 64 * (2 * c + 1):64 * (2 * c + 2)],
                                sab[:, 256:512], start=s0, stop=s1, **tp(0, qrow)),
                       lambda E, o, s0, s1, c=c, qrow=qrow, qcol=qcol:
                       E.matmul(o, Ss_all[qrow:qrow + 64, 64 * c:64 * (c + 1)],
                                XQ[qrow:qrow + 64, qcol + 256 * c:qcol + 256 * (c + 1)],
                                start=s0, stop=s1, **tp(qrow, qrow))]
                mm_group(bi, mms, deps=dsa_all + [dSs[c]], prows=64, cols=256, poff=qrow)
                oh_deps.append(read_bank(bi, "scalar", lambda E, c=c, bi=bi, qrow=qrow:
                                         E.activation(o_head[qrow:qrow + 64,
                                                             256 * c:256 * (c + 1)],
                                                      bk.sl(bi, 64, 256, poff=qrow), AF.Copy)))
        depst = b.emit("sync", lambda E, h=h:
                       E.dma_start(out=r_stB[:, :], in_=epsscr.ap()[h:h + 1, :]),
                       deps=[d_epsd] + ([eps_stg_war[0]] if eps_stg_war[0] else []),
                       inc="dEps", dma=True)
        dms_list = []
        for n in range(4):
            do2 = b.emit("vector", lambda E, n=n, qrow=qrow:
                         E.tensor_tensor(o2scr[qrow:qrow + 64, :],
                                         o_head[qrow:qrow + 64, 512 * n:512 * (n + 1)],
                                         o_head[qrow:qrow + 64, 512 * n:512 * (n + 1)],
                                         ALU.mult),
                         deps=oh_deps + dms_list[-1:], inc="dv")
            bi = 4 + (n % 2)
            mm_group(bi, [lambda E, o, s0, s1, qrow=qrow:
                          E.matmul(o, o64div[qrow:qrow + 64, :], o2scr[qrow:qrow + 64, :],
                                   start=s0, stop=s1, **tp(qrow, 0))],
                     deps=[do2], prows=1)
            dms_list.append(read_bank(bi, "vector", lambda E, n=n, bi=bi:
                                      E.tensor_tensor(r_stC[:, 512 * n:512 * (n + 1)],
                                                      bk.sl(bi, 1),
                                                      r_stB[:, 512 * n:512 * (n + 1)],
                                                      ALU.add),
                                      deps=[depst, d_mp]))
        dms = dms_list[-1]
        eps_stg_war[0] = dms
        dsq_ = b.emit("scalar", lambda E: E.activation(r32A[:, :], r_stC[:, :], AF.Sqrt),
                      deps=[dms, dDr], inc="sc")
        b.emit("vector", lambda E: E.drain(), deps=[dsq_])
        drc = b.emit("vector", lambda E: E.reciprocal(r32B[:, :], r32A[:, :]),
                     deps=[dsq_], inc="dv")
        for n in range(4):
            bi = 4 + (n % 2)
            mm_group(bi, [lambda E, o, s0, s1, n=n, qrow=qrow:
                          E.matmul(o, onesall[32:33, 0:64], r32B[:, 512 * n:512 * (n + 1)],
                                   start=s0, stop=s1, **tp(32, qrow))],
                     deps=[drc], prows=64, poff=qrow)
            o_done.append(read_bank(bi, "vector",
                                    lambda E, n=n, bi=bi, qrow=qrow, qcol=qcol:
                                    E.tensor_tensor(XQ[qrow:qrow + 64,
                                                       qcol + 512 * n:qcol + 512 * (n + 1)],
                                                    o_head[qrow:qrow + 64, 512 * n:512 * (n + 1)],
                                                    bk.sl(bi, 64, 512, poff=qrow), ALU.mult),
                                    deps=[tap_q_dep]))
        sw2_war[h % 2] = dprev
        hp_war[h % 2] = [hp["vtm"][-1], hp["kdfm"][-1]]
        kd_war[0] = oh_deps[-1]
        vtm_war[0] = oh_deps[-1]
    if "tap_o" in tap_t:
        b.emit("sync", lambda E: E.dma_start(out=tap_t["tap_o"].ap()[:], in_=XQ[:]),
               deps=o_done, inc="dT_o", dma=True)
    if upto < 8:
        return _ret()

    b.phase = "P6"

    p5_fin = [b.now("pe"), b.now("dv"), b.now("sc"), b.now("gp")]
    xq_hi = at("p6_xqh", [128, 4096], BF16, SCR0)
    xq_lo = at("p6_xql", [128, 4096], BF16, SCR0 + 8192)
    PHs = at("p6_ph", [128, 4096], BF16, SCR0 + 16384)
    PLs = at("p6_pl", [128, 4096], BF16, SCR0 + 24576)
    xtile = [at(f"p6_xt{i}", [128, 1024], F32, SCR0 + 32768 + 4096 * i) for i in range(2)]
    t1s = at("p6_t1", [128, 512], F32, SCR0 + 40960)
    pps = at("p6_pp", [128, 512], F32, SCR0 + 43008)
    outsb = [at(f"p6_ot{i}", [128, 512], F32, SCR0 + 45056 + 2048 * i) for i in range(2)]
    wstC2 = [at(f"p6_wst{i}", [128, 512], F32, 12288 + 2048 * 4 + 2048 * i)
             for i in range(2)]
    wstC_war = [None, None]
    dWg = load_wlo_only("wg", WGg_hi, WGg_lo, wstC2, wstC_war,
                        deps=[dWgh, d_sinv] + p5_fin)
    dWo = load_wlo_only("wo", WGo_hi, WGo_lo, wstC2, wstC_war,
                        deps=[dWoh, dWg] + p5_fin)
    acc_done = []
    xnq_war = [None]
    ph_war = [None]
    ot_war = [None, None]
    xt_war = [None, None]
    for n in range(NQ):
        dxh = b.emit("sync", lambda E, n=n: E.dma_start(out=xq_hi[:], in_=xnscr_hi.ap()[n]),
                     deps=xn_stash[-1:] + ([xnq_war[0]] if xnq_war[0] else p5_fin)
                     + b.tapdeps, inc="dXq6", dma=True)
        dxl = b.emit("sync", lambda E, n=n: E.dma_start(out=xq_lo[:], in_=xnscr_lo.ap()[n]),
                     deps=xn_stash[-1:] + ([xnq_war[0]] if xnq_war[0] else p5_fin),
                     inc="dXq6", dma=True)
        p_done = []
        t1_war = [None]
        pp_war = [None]
        for m in range(8):
            bi = m % 4
            mms = []
            for k in range(8):
                for wt, xt in ((WGg_hi, xq_hi), (WGg_hi, xq_lo), (WGg_lo, xq_hi)):
                    mms.append(lambda E, o, s0, s1, k=k, m=m, wt=wt, xt=xt:
                               E.matmul(o, wt[:, 1024 * k + 128 * m:1024 * k + 128 * (m + 1)],
                                        xt[:, 512 * k:512 * (k + 1)], start=s0, stop=s1))
            mm_group(bi, mms, deps=[dWg, dxh, dxl, o_done[-1]])
            dt1 = read_bank(bi, "scalar", lambda E, bi=bi:
                            E.activation(t1s[:], bk.sl(bi), AF.Silu),
                            deps=[pp_war[0]] if pp_war[0] else [])
            dpp = b.emit("vector", lambda E, m=m, n=n:
                         E.tensor_tensor(pps[:], t1s[:],
                                         XQ[:, 2048 * m + 512 * n:2048 * m + 512 * (n + 1)],
                                         ALU.mult),
                         deps=[dt1, o_done[-1]] + ([t1_war[0]] if t1_war[0] else []), inc="dv")
            dph = b.emit("scalar", lambda E, m=m:
                         E.activation(PHs[:, 512 * m:512 * (m + 1)], pps[:], AF.Copy),
                         deps=[dpp] + ([ph_war[0]] if ph_war[0] else []), inc="sc")
            dpl = b.emit("vector", lambda E, m=m:
                         E.tensor_tensor(PLs[:, 512 * m:512 * (m + 1)], pps[:],
                                         PHs[:, 512 * m:512 * (m + 1)], ALU.subtract),
                         deps=[dph], inc="dv")
            t1_war[0] = dpp
            pp_war[0] = dpl
            p_done += [dph, dpl]
        for tt in range(4):
            gtt = 4 * n + tt
            dxt = b.emit("sync", lambda E, gtt=gtt, tt=tt:
                         E.dma_start(out=xtile[tt % 2][:],
                                     in_=din["x"].ap()[128 * gtt:128 * (gtt + 1), :]),
                         deps=([xt_war[tt % 2]] if xt_war[tt % 2] else p5_fin) + b.tapdeps,
                         inc="dXt%d" % (tt % 2), dma=True)
            for oc in range(2):
                bi = 4 + 2 * (tt % 2) + oc
                mms = []
                for k in range(8):
                    for pt, wt in ((PHs, WGo_hi), (PLs, WGo_hi), (PHs, WGo_lo)):
                        mms.append(lambda E, o, s0, s1, k=k, tt=tt, oc=oc, pt=pt, wt=wt:
                                   E.matmul(o, pt[:, 512 * k + 128 * tt:512 * k + 128 * (tt + 1)],
                                            wt[:, 1024 * k + 512 * oc:1024 * k + 512 * (oc + 1)],
                                            start=s0, stop=s1))
                mm_group(bi, mms, deps=p_done + [dWo])
                dev = read_bank(bi, "vector", lambda E, tt=tt, oc=oc, bi=bi:
                                E.tensor_tensor(outsb[tt % 2][:], bk.sl(bi),
                                                xtile[tt % 2][:, 512 * oc:512 * (oc + 1)],
                                                ALU.add),
                                deps=[dxt] + ([ot_war[tt % 2]] if ot_war[tt % 2] else []))
                dst = b.emit("scalar", lambda E, gtt=gtt, oc=oc, tt=tt:
                             E.dma_start(out=accscr.ap()[128 * gtt:128 * (gtt + 1),
                                                         512 * oc:512 * (oc + 1)],
                                         in_=outsb[tt % 2][:]),
                             deps=[dev], inc="dAcc", dma=True)
                ot_war[tt % 2] = dst
                acc_done.append(dst)
            xt_war[tt % 2] = b.now("dv")
        xnq_war[0] = b.now("pe")
        ph_war[0] = b.now("pe")
    if "tap_acc" in tap_t:
        b.emit("sync", lambda E: E.dma_start(out=tap_t["tap_acc"].ap()[:], in_=accscr.ap()[:]),
               deps=[acc_done[-1]], inc="dT_acc", dma=True)
    if upto < 9:
        return _ret()

    b.phase = "P7"

    p6_fin = [b.now("pe"), b.now("dv"), b.now("sc"), b.now("gp"), acc_done[-1]]
    xnq_hi = at("p7_xh", [128, 4096], BF16, SCR0)
    xnq_lo = at("p7_xl", [128, 4096], BF16, SCR0 + 8192)
    wp1 = [at(f"p7_w1{i}", [128, 1024], BF16, SCR0 + 16384 + 2048 * i) for i in range(2)]
    wp2 = [at(f"p7_w2{i}", [128, 1024], BF16, SCR0 + 20480 + 2048 * i) for i in range(2)]
    hpan = [at(f"p7_hp{i}", [128, 4096], BF16, SCR0 + 24576 + 8192 * i) for i in range(2)]
    w3p = [at(f"p7_w3{i}", [128, 1024], BF16, SCR0 + 40960 + 2048 * i) for i in range(2)]
    jscr = at("p7_js", [128, 512], F32, SCR0 + 45056)
    jp = at("p7_jp", [128, 512], F32, SCR0 + 47104)
    acct = at("p7_acc", [128, 1024], F32, 12288)
    outt = [at(f"p7_o{i}", [128, 512], F32, 12288 + 4096 + 2048 * i) for i in range(2)]
    s3rep = rowpack[:, 1024:2048]

    ds3s = b.emit("sync", lambda E: E.dma_start(out=rowpack[0:1, 0:1024], in_=din["s3r"].ap()[:]),
                  deps=p6_fin + [d_sin, stg_war[0]], inc="dS3", dma=True)
    s3_deps = []
    for oc in range(2):
        bi = oc
        mm_group(bi, [lambda E, o, s0, s1, oc=oc:
                      E.matmul(o, onesall[0:1, 0:128], rowpack[0:1, 512 * oc:512 * (oc + 1)],
                               start=s0, stop=s1)], deps=[ds3s] + acc_done[-2:])
        s3_deps.append(read_bank(bi, "vector", lambda E, oc=oc, bi=bi:
                                 E.tensor_copy(s3rep[:, 512 * oc:512 * (oc + 1)], bk.sl(bi)),
                                 deps=p6_fin))

    HIN = (HA_hi, HA_lo)
    HOUT = (XN_hi, XN_lo)
    qglob_js_war = [acc_done[-1]]
    qglob_jp_war = [acc_done[-1]]
    out_final = []
    pwar = {"w1": [None, None], "w2": [None, None], "hp": [None, None], "w3": [None, None]}
    hwr = {"A": list(p6_fin), "B": list(p6_fin)}
    xq7_war = [None]
    acct_war = [None]
    out_war = [None, None]

    for q in range(NQ):
        dxh = b.emit("sync", lambda E, q=q: E.dma_start(out=xnq_hi[:], in_=xnscr_hi.ap()[q]),
                     deps=xn_stash[-1:] + ([xq7_war[0]] if xq7_war[0] else p6_fin)
                     + b.tapdeps, inc="dXq7", dma=True)
        dxl = b.emit("sync", lambda E, q=q: E.dma_start(out=xnq_lo[:], in_=xnscr_lo.ap()[q]),
                     deps=xn_stash[-1:] + ([xq7_war[0]] if xq7_war[0] else p6_fin),
                     inc="dXq7", dma=True)
        hin_done = []
        js_war = [qglob_js_war[0]]
        jp_war = [qglob_jp_war[0]]
        for m in range(32):
            pb1, pb2 = wp1[m % 2], wp2[m % 2]
            dw1 = b.emit("gpsimd", lambda E, m=m, pb1=pb1:
                         E.dma_start(out=pb1[:], in_=din["w1b"].ap()[m]),
                         deps=([pwar["w1"][m % 2]] if pwar["w1"][m % 2] else p6_fin),
                         inc="dPw1%d" % (m % 2), dma=True)
            dw2 = b.emit("gpsimd", lambda E, m=m, pb2=pb2:
                         E.dma_start(out=pb2[:], in_=din["w2b"].ap()[m]),
                         deps=([pwar["w2"][m % 2]] if pwar["w2"][m % 2] else p6_fin),
                         inc="dPw2%d" % (m % 2), dma=True)
            bi1, bi2 = m % 2, 2 + m % 2
            mms1, mms2 = [], []
            for k in range(8):
                for xt in (xnq_hi, xnq_lo):
                    mms1.append(lambda E, o, s0, s1, k=k, pb1=pb1, xt=xt:
                                E.matmul(o, pb1[:, 128 * k:128 * (k + 1)],
                                         xt[:, 512 * k:512 * (k + 1)], start=s0, stop=s1))
                    mms2.append(lambda E, o, s0, s1, k=k, pb2=pb2, xt=xt:
                                E.matmul(o, pb2[:, 128 * k:128 * (k + 1)],
                                         xt[:, 512 * k:512 * (k + 1)], start=s0, stop=s1))
            d1_ = mm_group(bi1, mms1, deps=[dw1, dxh, dxl])
            d2_ = mm_group(bi2, mms2, deps=[dw2, dxh, dxl])
            pwar["w1"][m % 2] = d1_
            pwar["w2"][m % 2] = d2_
            dt1 = read_bank(bi1, "scalar", lambda E, m=m, bi1=bi1:
                            E.activation(jscr[:], bk.sl(bi1), AF.Silu, scale=s1h[:, m:m + 1]),
                            deps=[js_war[0]] if js_war[0] else [])
            dpp = read_bank(bi2, "vector", lambda E, m=m, bi2=bi2:
                            E.scalar_tensor_tensor(jp[:], bk.sl(bi2), s2h[:, m:m + 1],
                                                   jscr[:], ALU.mult, ALU.mult),
                            deps=[dt1] + ([jp_war[0]] if jp_war[0] else []))
            dh = b.emit("scalar", lambda E, m=m:
                        E.activation(HIN[0][:, 512 * m:512 * (m + 1)], jp[:], AF.Copy),
                        deps=[dpp] + [d for d in hwr["A"] if d], inc="sc")
            dl = b.emit("vector", lambda E, m=m:
                        E.tensor_tensor(HIN[1][:, 512 * m:512 * (m + 1)], jp[:],
                                        HIN[0][:, 512 * m:512 * (m + 1)], ALU.subtract),
                        deps=[dh], inc="dv")
            js_war[0] = dpp
            jp_war[0] = dl
            hin_done += [dh, dl]
        hwr["A"] = [None]
        for stage, wname, sc_ap, src, dst, dwkey in (
                ("B", "h1b", sh1h, HIN, HOUT, "B"),
                ("C", "h2b", sh2h, HOUT, HIN, "A")):
            sdone = []
            hjs_war = [None]
            for m in range(32):
                pb = hpan[m % 2]
                dw = b.emit("gpsimd", lambda E, m=m, pb=pb, wname=wname:
                            E.dma_start(out=pb[:], in_=din[wname].ap()[m]),
                            deps=([pwar["hp"][m % 2]] if pwar["hp"][m % 2] else p6_fin),
                            inc="dPh%d" % (m % 2), dma=True)
                bi = m % 2
                mms = []
                for k in range(32):
                    for xt in (src[0], src[1]):
                        mms.append(lambda E, o, s0, s1, k=k, pb=pb, xt=xt:
                                   E.matmul(o, pb[:, 128 * k:128 * (k + 1)],
                                            xt[:, 512 * k:512 * (k + 1)], start=s0, stop=s1))
                dmm = mm_group(bi, mms, deps=([dw] + hin_done) if m < 2 else [dw])
                pwar["hp"][m % 2] = dmm
                dt = read_bank(bi, "scalar", lambda E, m=m, bi=bi, sc_ap=sc_ap:
                               E.activation(jscr[:], bk.sl(bi), AF.Gelu, scale=sc_ap[:, m:m + 1]),
                               deps=[hjs_war[0]] if hjs_war[0] else [js_war[0]])
                dh = b.emit("scalar", lambda E, m=m, dst=dst:
                            E.activation(dst[0][:, 512 * m:512 * (m + 1)], jscr[:], AF.Copy),
                            deps=[dt] + [d for d in hwr[dwkey] if d], inc="sc")
                dl = b.emit("vector", lambda E, m=m, dst=dst:
                            E.tensor_tensor(dst[1][:, 512 * m:512 * (m + 1)], jscr[:],
                                            dst[0][:, 512 * m:512 * (m + 1)], ALU.subtract),
                            deps=[dh, jp_war[0]], inc="dv")
                hjs_war[0] = dl
                sdone += [dh, dl]
            hwr[dwkey] = [None]
            hin_done = sdone
        w3jp_war = [jp_war[0]]
        for k in range(32):
            pb = w3p[k % 2]
            dw = b.emit("gpsimd", lambda E, k=k, pb=pb:
                        E.dma_start(out=pb[:], in_=din["w3b"].ap()[k]),
                        deps=([pwar["w3"][k % 2]] if pwar["w3"][k % 2] else p6_fin),
                        inc="dPw3%d" % (k % 2), dma=True)
            last = None
            for tt in range(4):
                for oc in range(2):
                    bi = 2 * tt + oc
                    outap = bk.sl(bi)
                    alldeps = [dw] + (hin_done if k == 0 else [])
                    if k == 0 and bk.rdep[bi] is not None:
                        alldeps.append(bk.rdep[bi])
                    for pi, pt in enumerate((HIN[0], HIN[1])):
                        first = (k == 0 and pi == 0)
                        firstk = (tt == 0 and oc == 0 and pi == 0)
                        lastf = (k == 31 and pi == 1)
                        lastk = (tt == 3 and oc == 1 and pi == 1)
                        last = b.emit("tensor",
                                      lambda E, o=outap, k=k, tt=tt, oc=oc, pt=pt, pb=pb,
                                      first=first, lastf=lastf:
                                      E.matmul(o, pt[:, 512 * k + 128 * tt:512 * k + 128 * (tt + 1)],
                                               pb[:, 512 * oc:512 * (oc + 1)],
                                               start=first, stop=lastf),
                                      deps=alldeps if firstk else (),
                                      inc="pe" if (lastf or lastk) else None)
                    if k == 31:
                        bk.wdep[bi] = last
            pwar["w3"][k % 2] = last
        for tt in range(4):
            gtt = 4 * q + tt
            dac = b.emit("sync", lambda E, gtt=gtt:
                         E.dma_start(out=acct[:], in_=accscr.ap()[128 * gtt:128 * (gtt + 1), :]),
                         deps=[acc_done[-1]] + ([acct_war[0]] if acct_war[0] else p6_fin),
                         inc="dAcct", dma=True)
            for oc in range(2):
                bi = 2 * tt + oc
                dt = read_bank(bi, "vector", lambda E, bi=bi, oc=oc:
                               E.tensor_tensor(jp[:], bk.sl(bi), s3rep[:, 512 * oc:512 * (oc + 1)],
                                               ALU.mult),
                               deps=s3_deps + ([out_war[oc]] if out_war[oc] else [])
                               + [w3jp_war[0]])
                dt2 = b.emit("vector", lambda E, oc=oc:
                             E.tensor_tensor(outt[oc][:], jp[:], acct[:, 512 * oc:512 * (oc + 1)],
                                             ALU.add), deps=[dt, dac], inc="dv")
                w3jp_war[0] = dt2
                dst = b.emit("scalar", lambda E, gtt=gtt, oc=oc:
                             E.dma_start(out=out.ap()[128 * gtt:128 * (gtt + 1),
                                                      512 * oc:512 * (oc + 1)],
                                         in_=outt[oc][:]),
                             deps=[dt2], inc="dOut%d" % oc, dma=True)
                out_war[oc] = dst
                out_final.append(dst)
            acct_war[0] = b.now("dv")
        xq7_war[0] = b.now("pe")
        hwr["A"] = [b.now("pe")]
        hwr["B"] = [b.now("pe")]
        qglob_js_war[0] = b.now("sc")
        qglob_jp_war[0] = b.now("dv")

    b.wait_only("sync", [out_final[-1], out_final[-2]] + b.tapdeps)
    return _finish(nc, ctx, b)


def _finish(nc, ctx, b):
    def _replay(E, eng):
        for item in b.plan:
            if item[0] == eng:
                with nc.named_scope(item[2]):
                    item[1](E)

    with nc.Block() as block:
        @block.tensor
        def _(E):
            _replay(E, "tensor")

        @block.vector
        def _(E):
            _replay(E, "vector")

        @block.scalar
        def _(E):
            _replay(E, "scalar")

        @block.gpsimd
        def _(E):
            _replay(E, "gpsimd")

        @block.sync
        def _(E):
            _replay(E, "sync")
    ctx.close()
    return nc


def kernel(**inputs):
    in_maps = host_prep(inputs)
    nc = build()
    res = run_bass_kernel_spmd(nc, in_maps, core_ids=list(range(NCORES)))
    outs = [res.results[c]["out"] for c in range(NCORES)]
    return np.concatenate(outs, axis=0).reshape(2, 8192, D).astype(np.float32)
